# version 41
# baseline (speedup 1.0000x reference)
import numpy as np
import concourse.bass as bass
import concourse.mybir as mybir
from concourse.bass_utils import run_bass_kernel_spmd
from contextlib import ExitStack

F32 = mybir.dt.float32
BF16 = mybir.dt.bfloat16
I8 = mybir.dt.int8
AF = mybir.ActivationFunctionType
ALU = mybir.AluOpType

NCORES = 8
T = 1024
DM = 2048
CD = DM // 128
IN = 8192
NI = IN // 128
NOG = IN // 256
NH = 16
S = NCORES * T
SCALE = 1.0 / float(np.sqrt(128.0))
LN_EPS = 1e-5

_BUILD_UPTO = 4

KOFF = 16384
POFF = 24576
PAOFF = 28672


class PlanSem:
    __slots__ = ("h", "n")

    def __init__(self):
        self.h = None
        self.n = 0

    def inc(self, inst, k=1):
        self.n += k
        return self.n


class Sem:
    def __init__(self, stack, nc, name):
        self.h = stack.enter_context(nc.semaphore(name))
        self.n = 0

    def inc(self, inst, k=1):
        inst.then_inc(self.h, k)
        self.n += k
        return self.n


class _FakeInst:
    def then_inc(self, *a, **k):
        return self


class _FakeEng:
    def __getattr__(self, name):
        def f(*a, **k):
            return _FakeInst()
        return f


class _FakeBlock:
    def __getattr__(self, name):
        def runner(f):
            f(_FakeEng())
        return runner


def emit_all(block, nc, tensors, sems, rec, real):
    (x_res, y_bf, slabA, slabW, ln_scr, stats, gtmp, sg_s, su_s, sd_s,
     sdh_s, ones_b, ones_f, onesk1, onesk1b, cst, pb, pp,
     x_d, wqkvT_d, woutT_d, wg_d, wu_d, wd_d, sg_d, su_d, sd_d, out_d,
     k_ins, k_outs, v_ins, v_outs) = tensors
    (SX, SC, KB, VA, VB, VC, VD, WA, WB, QA, QB, DA, DB, DC, DD,
     KR0, KR1, KR2, VR0, VR1, VR2, CC, PE, AC, DV, GP, KN, GX,
     QT0, QT1, UA, UB, KM, VG) = sems
    QT = [QT0, QT1]
    US = [UA, UB]
    VI = [VA, VB, VC, VD]
    KR = [KR0, KR1, KR2]
    VR = [VR0, VR1, VR2]
    WS = [WA, WB]
    QS = [QA, QB]
    DS = [DA, DB, DC, DD]

    def W(eng, sem, key):
        if real:
            eng.wait_ge(sem.h, rec[key])

    def R(key, val):
        if real:
            assert rec[key] == val, (key, rec[key], val)
        rec[key] = val

    bank_free = {}
    ring_last = {}

    def ring_guard(eng, sem, name, key):
        prev = ring_last.get(name)
        if prev is not None:
            W(eng, sem, prev)
        ring_last[name] = key

    def use_bank(eng, i):
        if i in bank_free and bank_free[i] is not None:
            sem, key = bank_free[i]
            W(eng, sem, key)
            bank_free[i] = None

    def xload(eng, g, sem):
        src = x_d.ap()[g * 512:(g + 1) * 512, :].rearrange(
            "(c p) t -> p c t", p=128)
        dst = x_res[:, g * 4 * T:(g + 1) * 4 * T].rearrange(
            "p (c t) -> p c t", c=4)
        sem.inc(eng.dma_start(out=dst, in_=src), 16)
        if g % 2 == 1:
            R(("xg", g // 2), sem.n)

    def sec_init_sync(sy):
        xload(sy, 0, SX)
        xload(sy, 1, SX)
        SC.inc(sy.dma_start(out=sg_s[:], in_=sg_d.ap()), 16)
        SC.inc(sy.dma_start(out=su_s[:], in_=su_d.ap()), 16)
        SC.inc(sy.dma_start(out=sd_s[:], in_=sd_d.ap()), 16)
        R("scales", SC.n)

    block.sync(sec_init_sync)

    def sec_init_gp(g):
        xload(g, 2, GX)
        xload(g, 3, GX)

    block.gpsimd(sec_init_gp)

    def sec_init_vec(v):
        v.memset(ones_b[:], 1.0)
        v.memset(ones_f[:], 1.0)
        v.memset(onesk1[:], 1.0)
        v.memset(onesk1b[:], 1.0)
        v.memset(cst[0:1, 0:1], LN_EPS)
        v.memset(cst[0:1, 1:2], 1e-8)
        W(v, SC, "scales")
        DV.inc(v.tensor_scalar(sdh_s[:], sd_s[:], 0.5, None, ALU.mult))
        R("init_dv", DV.n)

    block.vector(sec_init_vec)

    def emit_ln(tag, x_ready, y_free):
        bcf = slabA[:, :].bitcast(F32)

        def bcA(th):
            return bcf[:, 14336 + th * 512:14336 + th * 512 + 512]

        def bcB(th):
            return bcf[:, 15360 + th * 512:15360 + th * 512 + 512]
        def sec_sq(s):
            for c in range(CD):
                if x_ready is not None:
                    sv = x_ready(c)
                    if sv is not None:
                        W(s, sv[0], sv[1])
                for th in (0, 1):
                    if c >= 1:
                        W(s, PE, ("lnst2", tag, c - 1, th))
                    AC.inc(s.activation(
                        gtmp[:, 512 * th:512 * th + 512],
                        x_res[:, c * T + 512 * th:c * T + 512 * th + 512],
                        AF.Square))
                    R(("lnsq2", tag, c, th), AC.n)

        def sec_cp(v):
            xbf = ln_scr[:, :].bitcast(BF16)
            for c in range(CD):
                if x_ready is not None:
                    sv = x_ready(c)
                    if sv is not None:
                        W(v, sv[0], sv[1])
                for th in (0, 1):
                    sl = (c % 2) * 2 + th
                    if c >= 2:
                        W(v, PE, ("lnst2", tag, c - 2, th))
                    DV.inc(v.tensor_copy(
                        xbf[:, sl * 512:sl * 512 + 512],
                        x_res[:, c * T + 512 * th:c * T + 512 * th + 512]))
                    R(("lncp", tag, c, th), DV.n)

        def sec_rstd(s):
            for th in (0, 1):
                W(s, DV, ("lnvar", tag, th))
                if real:
                    s.wait_ge(AC.h, AC.n)
                AC.inc(s.activation(ln_scr[0:1, 512:1024],
                                    stats[0:1, 1024 + 512 * th:1536 + 512 * th],
                                    AF.Ln, bias=cst[0:1, 0:1]))
                if real:
                    s.wait_ge(AC.h, AC.n)
                AC.inc(s.activation(
                    stats[0:1, :].bitcast(BF16)[0:1, 2048 + 512 * th:
                                                2560 + 512 * th],
                    ln_scr[0:1, 512:1024], AF.Exp, scale=-0.5))
                R(("lnrstd", tag, th), AC.n)
            for th in (0, 1):
                W(s, PE, ("lnbc", tag, th))
                AC.inc(s.activation(bcA(th), pb[4 + th][:, :], AF.Copy))
                AC.inc(s.activation(bcB(th), pb[6 + th][:, :], AF.Copy))
                R(("lncpy", tag, th), AC.n)

        block.vector(sec_cp)
        block.scalar(sec_sq)
        block.scalar(sec_rstd)

        def sec_mm(t):
            W(t, DV, "init_dv")
            for i in range(4):
                use_bank(t, i)
            xbf = ln_scr[:, :].bitcast(BF16)
            for c in range(CD):
                for th in (0, 1):
                    sl = (c % 2) * 2 + th
                    W(t, AC, ("lnsq2", tag, c, th))
                    W(t, DV, ("lncp", tag, c, th))
                    t.matmul(pb[th][0:1, :], ones_b[:, 0:1],
                             xbf[:, sl * 512:sl * 512 + 512],
                             start=(c == 0), stop=(c == CD - 1))
                    ins = t.matmul(pb[2 + th][0:1, :], ones_b[:, 0:1],
                                   gtmp[:, 512 * th:512 * th + 512],
                                   start=(c == 0), stop=(c == CD - 1))
                    PE.inc(ins)
                    R(("lnst2", tag, c, th), PE.n)
                R(("lnst", tag, c), PE.n)
            sbf = stats[0:1, :].bitcast(BF16)
            for th in (0, 1):
                use_bank(t, 4 + th)
                use_bank(t, 6 + th)
                W(t, DV, ("lnm", tag, th))
                t.matmul(pb[4 + th][:, :], onesk1b[0:1, :],
                         slabA[0:1, 27648 + 512 * th:27648 + 512 * th + 512],
                         start=True, stop=True)
                W(t, AC, ("lnrstd", tag, th))
                PE.inc(t.matmul(pb[6 + th][:, :], onesk1b[0:1, :],
                                sbf[0:1, 2048 + 512 * th:2560 + 512 * th],
                                start=True, stop=True))
                R(("lnbc", tag, th), PE.n)

        block.tensor(sec_mm)

        def sec_vec(v):
            mbf = slabA[0:1, :]
            W(v, PE, ("lnst", tag, CD - 1))
            for th in (0, 1):
                DV.inc(v.tensor_scalar(
                    mbf[0:1, 27648 + 512 * th:27648 + 512 * th + 512],
                    pb[th][0:1, :], 1.0 / DM, None, ALU.mult))
                R(("lnm", tag, th), DV.n)
            for th in (0, 1):
                DV.inc(v.tensor_scalar(stats[0:1, 1024 + 512 * th:1536 + 512 * th],
                                       pb[2 + th][0:1, :], 1.0 / DM, None,
                                       ALU.mult))
            R(("lnmsq", tag), DV.n)
            for th in (0, 1):
                if real:
                    v.wait_ge(DV.h, DV.n)
                DV.inc(v.tensor_tensor(
                    stats[0:1, 0:512],
                    mbf[0:1, 27648 + 512 * th:27648 + 512 * th + 512],
                    mbf[0:1, 27648 + 512 * th:27648 + 512 * th + 512],
                    ALU.mult))
                if real:
                    v.wait_ge(DV.h, DV.n)
                DV.inc(v.tensor_tensor(stats[0:1, 1024 + 512 * th:1536 + 512 * th],
                                       stats[0:1, 1024 + 512 * th:1536 + 512 * th],
                                       stats[0:1, 0:512], ALU.subtract))
                R(("lnvar", tag, th), DV.n)
            first = True
            for th in (0, 1):
                W(v, AC, ("lncpy", tag, th))
                for c in range(CD):
                    if th == 0 and c % 2 == 1:
                        continue
                    if first and y_free is not None:
                        W(v, y_free[0], y_free[1])
                    first = False
                    scr = ln_scr[:, (c % 2) * 512:(c % 2) * 512 + 512]
                    if real:
                        v.wait_ge(DV.h, DV.n)
                    DV.inc(v.tensor_tensor(
                        scr,
                        x_res[:, c * T + 512 * th:c * T + 512 * th + 512],
                        bcA(th), ALU.subtract))
                    if real:
                        v.wait_ge(DV.h, DV.n)
                    DV.inc(v.tensor_tensor(
                        y_bf[:, c * T + 512 * th:c * T + 512 * th + 512],
                        scr, bcB(th), ALU.mult))
                    if th == 0:
                        R(("yac", tag, c), DV.n)
                R(("y", tag, th), DV.n)

        block.vector(sec_vec)

        def gp_apply(g):
            if y_free is not None:
                W(g, y_free[0], y_free[1])
            W(g, AC, ("lncpy", tag, 0))
            gscr = gtmp[:, :].bitcast(F32)
            for c in range(1, CD, 2):
                if real:
                    g.wait_ge(GP.h, GP.n)
                GP.inc(g.tensor_tensor(
                    gscr[:, 0:512],
                    x_res[:, c * T:c * T + 512],
                    bcA(0), ALU.subtract))
                if real:
                    g.wait_ge(GP.h, GP.n)
                GP.inc(g.tensor_tensor(
                    y_bf[:, c * T:c * T + 512],
                    gscr[:, 0:512], bcB(0), ALU.mult))
                R(("yac", tag, c), GP.n)
            R(("yg0", tag), GP.n)
        for i in range(4):
            bank_free[i] = (DV, ("lnmsq", tag))
        for i in range(4, 8):
            bank_free[i] = (AC, ("lncpy", tag, 1))
        return gp_apply

    def emit_ffn(tag, ytag, w_free, skip_first_gates=False, gp_pre=None):
        def sec_gp(g):
            if w_free is not None:
                W(g, w_free[0], w_free[1])
            if skip_first_gates:
                for jj, key in ((1, ("omm", 6, 1, 1)), (0, ("omm", 7, 1, 1))):
                    W(g, PE, key)
                    par = jj % 2
                    ring_guard(g, US[par], ("us", par), ("fg", tag, 0, jj))
                    for cg in range(4):
                        src = wu_d.ap()[cg * 512:(cg + 1) * 512,
                                        jj * 256:(jj + 1) * 256].rearrange(
                            "(c p) o -> p c o", p=128)
                        dst = slabW[:, 8192 + par * 4096 + cg * 1024:
                                    8192 + par * 4096 + cg * 1024 + 1024
                                    ].rearrange("p (c o) -> p c o", c=4)
                        US[par].inc(g.dma_start(out=dst, in_=src), 16)
                    R(("fg", tag, 0, jj), US[par].n)
                if gp_pre is not None:
                    gp_pre(g)
            for th in (0, 1):
                for j in range(NOG):
                    lin = th * NOG + j
                    if skip_first_gates and lin < 2:
                        continue
                    par = j % 2
                    if th == 1 and j <= 1:
                        W(g, PE, ("fdmm", tag, 0, 1, 61 if j == 0 else 63))
                        ring_guard(g, WS[par], ("ws", par),
                                   ("fgG", tag, th, j))
                        for cg in range(4):
                            src = wg_d.ap()[cg * 512:(cg + 1) * 512,
                                            j * 256:(j + 1) * 256].rearrange(
                                "(c p) o -> p c o", p=128)
                            dst = slabW[:, par * 4096 + cg * 1024:
                                        par * 4096 + cg * 1024 + 1024
                                        ].rearrange("p (c o) -> p c o", c=4)
                            WS[par].inc(g.dma_start(out=dst, in_=src), 16)
                        R(("fgG", tag, th, j), WS[par].n)
                        continue
                    elif lin >= 2:
                        pth, pj = divmod(lin - 2, NOG)
                        W(g, PE, ("fup", tag, pth, pj, 1))
                    for w_dd, off in ((wg_d, 0), (wu_d, 8192)):
                        wsem = WS[par] if w_dd is wg_d else US[par]
                        if w_dd is wg_d:
                            ring_guard(g, wsem, ("ws", par),
                                       ("fgG", tag, th, j))
                        else:
                            ring_guard(g, wsem, ("us", par),
                                       ("fg", tag, th, j))
                        for cg in range(4):
                            src = w_dd.ap()[cg * 512:(cg + 1) * 512,
                                            j * 256:(j + 1) * 256].rearrange(
                                "(c p) o -> p c o", p=128)
                            dst = slabW[:, off + par * 4096 + cg * 1024:
                                        off + par * 4096 + cg * 1024 + 1024
                                        ].rearrange("p (c o) -> p c o", c=4)
                            wsem.inc(g.dma_start(out=dst, in_=src), 16)
                        if w_dd is wg_d:
                            R(("fgG", tag, th, j), wsem.n)
                    R(("fg", tag, th, j), US[par].n)
                    if gp_pre is not None and th == 0 and j == 1:
                        gp_pre(g)
                def up_half(jj):
                    W(g, PE, ("fup", tag, 0, NOG - 2 + jj, 1))
                    par = jj % 2
                    ring_guard(g, US[par], ("us", par), ("fg", tag, 1, jj))
                    for cg in range(4):
                        src = wu_d.ap()[cg * 512:(cg + 1) * 512,
                                        jj * 256:(jj + 1) * 256].rearrange(
                            "(c p) o -> p c o", p=128)
                        dst = slabW[:, 8192 + par * 4096 + cg * 1024:
                                    8192 + par * 4096 + cg * 1024 + 1024
                                    ].rearrange("p (c o) -> p c o", c=4)
                        US[par].inc(g.dma_start(out=dst, in_=src), 16)
                    R(("fg", tag, 1, jj), US[par].n)
                for oh in (0, 1):
                    for i in range(NI):
                        if th == 0 and oh == 0 and i in (2, 4):
                            up_half(0 if i == 2 else 1)
                        dk = th * 128 + oh * 64 + i
                        if oh == 0 and i < 4:
                            W(g, PE, ("fgate", tag, th,
                                      NOG - 2 if i < 2 else NOG - 1, 1))
                        if dk >= 4:
                            pth, r = divmod(dk - 4, 128)
                            poh, pi = divmod(r, 64)
                            W(g, PE, ("fdmm", tag, pth, poh, pi))
                        par = i % 4
                        ring_guard(g, DS[par], ("ds", par),
                                   ("fdg", tag, th, oh, i))
                        DS[par].inc(g.dma_start(
                            out=slabW[:, par * 2048:(par + 1) * 2048],
                            in_=wd_d.ap()[i * 128:(i + 1) * 128, :]), 16)
                        R(("fdg", tag, th, oh, i), DS[par].n)

        block.gpsimd(sec_gp)

        def sec_mm(t):
            for th in (0, 1):
                if th == 1:
                    W(t, DV, ("y", ytag, th))
                for j in range(NOG):
                    par = j % 2
                    lin = th * NOG + j
                    W(t, WS[par], ("fgG", tag, th, j))
                    for otl in (0, 1):
                        b = 2 * par + otl
                        if lin < 2:
                            use_bank(t, b)
                        else:
                            pth, pj = divmod(lin - 2, NOG)
                            W(t, AC, ("fsilu", tag, pth, pj, otl))
                        if th == 1 and j <= 1:
                            W(t, DV, ("ffnx", tag, 0, 1, b))
                        ins = None
                        for c in range(CD):
                            if th == 0 and j == 0 and otl == 0:
                                W(t, DV if c % 2 == 0 else GP,
                                  ("yac", ytag, c))
                            ins = t.matmul(
                                pb[b][:, :],
                                slabW[:, par * 4096 + c * 256 + otl * 128:
                                      par * 4096 + c * 256 + otl * 128 + 128],
                                y_bf[:, c * T + 512 * th:c * T + 512 * th + 512],
                                start=(c == 0), stop=(c == CD - 1))
                        PE.inc(ins)
                        R(("fgate", tag, th, j, otl), PE.n)
                    W(t, US[par], ("fg", tag, th, j))
                    for otl in (0, 1):
                        b = 4 + 2 * par + otl
                        if lin < 2:
                            use_bank(t, b)
                        else:
                            pth, pj = divmod(lin - 2, NOG)
                            W(t, DV, ("ffuse", tag, pth, pj, otl))
                        if th == 1 and j <= 1:
                            W(t, DV, ("ffnx", tag, 0, 1, b))
                        ins = None
                        for c in range(CD):
                            ins = t.matmul(
                                pb[b][:, :],
                                slabW[:, 8192 + par * 4096 + c * 256 + otl * 128:
                                      8192 + par * 4096 + c * 256 + otl * 128 + 128],
                                y_bf[:, c * T + 512 * th:c * T + 512 * th + 512],
                                start=(c == 0), stop=(c == CD - 1))
                        PE.inc(ins)
                        R(("fup", tag, th, j, otl), PE.n)
                W(t, AC, ("fsilu", tag, th, NOG - 1, 1))
                W(t, DV, ("ffuse", tag, th, NOG - 1, 1))
                if th == 0:
                    for b in range(8):
                        use_bank(t, b)
                for oh in (0, 1):
                    for i in range(NI):
                        par = i % 4
                        W(t, DS[par], ("fdg", tag, th, oh, i))
                        ins = None
                        for ot in range(8):
                            if i == 0 and (oh == 1 or th == 1):
                                pth = th if oh == 1 else th - 1
                                W(t, DV, ("ffnx", tag, pth,
                                          0 if oh == 1 else 1, ot))
                            ins = t.matmul(
                                pb[ot][:, :],
                                slabW[:, par * 2048 + oh * 1024 + ot * 128:
                                      par * 2048 + oh * 1024 + ot * 128 + 128],
                                slabA[:, i * 512:(i + 1) * 512],
                                start=(i == 0), stop=(i == NI - 1))
                            if i == NI - 1:
                                PE.inc(ins)
                                R(("fdmm2", tag, th, oh, ot), PE.n)
                        if i < NI - 1:
                            PE.inc(ins)
                        R(("fdmm", tag, th, oh, i), PE.n)
            R(("ffnw", tag), PE.n)
            R(("ffny", tag), rec[("fup", tag, 1, NOG - 1, 1)])

        block.tensor(sec_mm)

        def sec_act(s):
            W(s, SC, "scales")
            for th in (0, 1):
                for j in range(NOG):
                    par = j % 2
                    lin = th * NOG + j
                    for otl in (0, 1):
                        ot = j * 2 + otl
                        b = 2 * par + otl
                        W(s, PE, ("fgate", tag, th, j, otl))
                        if lin >= 1:
                            pth, pj = divmod(lin - 1, NOG)
                            W(s, DV, ("ffuse", tag, pth, pj, otl))
                        AC.inc(s.activation(gtmp[:, otl * 512:otl * 512 + 512],
                                            pb[b][:, :], AF.Silu,
                                            scale=sg_s[:, ot:ot + 1]))
                        R(("fsilu", tag, th, j, otl), AC.n)

        block.scalar(sec_act)

        def sec_vec(v):
            for th in (0, 1):
                for j in range(NOG):
                    par = j % 2
                    for otl in (0, 1):
                        ot = j * 2 + otl
                        b = 4 + 2 * par + otl
                        W(v, AC, ("fsilu", tag, th, j, otl))
                        W(v, PE, ("fup", tag, th, j, otl))
                        DV.inc(v.scalar_tensor_tensor(
                            slabA[:, ot * 512:(ot + 1) * 512],
                            pb[b][:, :], su_s[:, ot:ot + 1],
                            gtmp[:, otl * 512:otl * 512 + 512],
                            ALU.mult, ALU.mult))
                        R(("ffuse", tag, th, j, otl), DV.n)
                for oh in (0, 1):
                    for ot in range(8):
                        c = oh * 8 + ot
                        W(v, PE, ("fdmm2", tag, th, oh, ot))
                        DV.inc(v.scalar_tensor_tensor(
                            x_res[:, c * T + 512 * th:c * T + 512 * th + 512],
                            pb[ot][:, :], sdh_s[:, c:c + 1],
                            x_res[:, c * T + 512 * th:c * T + 512 * th + 512],
                            ALU.mult, ALU.add))
                        R(("ffnx", tag, th, oh, ot), DV.n)

        block.vector(sec_vec)
        for i in range(8):
            bank_free[i] = (DV, ("ffnx", tag, 1, 1, 7))

    def emit_qkv(ytag, w_free, gp_pre=None):
        def vgran_dma(eng, d, sem=None, cgs=range(4), record=True,
                      dbuf=None, doff=None):
            par = d % 2
            if sem is None:
                sem = QS[par]
            if dbuf is None:
                dbuf, doff = slabW, par * 8192
            for cg in cgs:
                src = wqkvT_d.ap()[cg * 512:(cg + 1) * 512,
                                   4096 + d * 512:4096 + d * 512 + 512].rearrange(
                    "(c p) o -> p c o", p=128)
                dst = dbuf[:, doff + cg * 2048:
                           doff + cg * 2048 + 2048
                           ].rearrange("p (c o) -> p c o", c=4)
                sem.inc(eng.dma_start(out=dst, in_=src), 16)
            if record:
                R(("vg", d), sem.n)
            return sem

        def sec_gp(g):
            jlist = list(range(8, 16)) + list(range(8))

            def qgran(idx):
                j = jlist[idx]
                if idx == 0:
                    W(g, PE, ("fup", "f1", 1, NOG - 2, 1))
                elif idx == 1:
                    W(g, PE, ("fup", "f1", 1, NOG - 1, 1))
                else:
                    W(g, PE, ("qkmm", jlist[idx - 2], 1, 1))
                par = idx % 2
                if idx >= 2:
                    W(g, QT[par], ("qgH", jlist[idx - 2]))
                    W(g, QS[par], ("qg", jlist[idx - 2]))
                for cg in range(4):
                    src = wqkvT_d.ap()[cg * 512:(cg + 1) * 512,
                                       j * 256:(j + 1) * 256].rearrange(
                        "(c p) o -> p c o", p=128)
                    dst = slabW[:, 8192 + par * 4096 + cg * 1024:
                                8192 + par * 4096 + cg * 1024 + 1024
                                ].rearrange("p (c o) -> p c o", c=4)
                    (QT[par] if cg < 2 else QS[par]).inc(
                        g.dma_start(out=dst, in_=src), 16)
                    if cg == 1:
                        R(("qgH", j), QT[par].n)
                R(("qg", j), QS[par].n)

            def vgather(d):
                W(g, VI[d], ("vin", 2 * d + 1))
                CC.inc(g.collective_compute(
                    "AllGather", ALU.bypass,
                    replica_groups=[list(range(NCORES))],
                    ins=[v_ins[d].ap()], outs=[v_outs[d].ap()]))

            def kgather(g2):
                W(g, KN if g2 == 0 else KM, ("kin", g2))
                CC.inc(g.collective_compute(
                    "AllGather", ALU.bypass,
                    replica_groups=[list(range(NCORES))],
                    ins=[k_ins[g2].ap()], outs=[k_outs[g2].ap()]))

            vstage = ln_scr[:, :].bitcast(BF16)

            def vbounce(u):
                d, ttg = divmod(u, 2)
                W(g, DV, ("ve", u, 3))
                if u % 2 == 1:
                    W(g, VI[d], ("vin", u - 1))
                dst = v_ins[d].ap()[ttg * 512:(ttg + 1) * 512, :].rearrange(
                    "(tt p) d2 -> p tt d2", p=128)
                vsrc = vstage[:, 0:2048].rearrange(
                    "p (tt d2) -> p tt d2", tt=4)
                VI[d].inc(g.dma_start(out=dst, in_=vsrc), 16)
                R(("vin", u), VI[d].n)

            qgran(0)
            qgran(1)
            if gp_pre is not None:
                gp_pre(g)
            qgran(2)
            qgran(3)
            W(g, PE, ("fdmm", "f1", 1, 1, NI - 1))
            vgran_dma(g, 0, sem=KB, cgs=(0, 1), record=False)
            qgran(4)
            qgran(5)
            vgran_dma(g, 0, sem=KB, cgs=(2, 3))
            for idx in range(6, 10):
                qgran(idx)
            kgather(0)
            W(g, PE, ("vmm", 1, 3))
            W(g, KB, ("vg", 0))
            vgran_dma(g, 2, sem=KB)
            for idx in range(10, 16):
                qgran(idx)
            vgather(0)
            kgather(1)
            W(g, KM, ("kin", 1))
            vgran_dma(g, 1, sem=VG, dbuf=slabA, doff=24576)
            W(g, PE, ("qkmm", jlist[15], 1, 1))
            W(g, KB, ("vg", 2))
            vgran_dma(g, 3, sem=KB)
            vbounce(2)
            vbounce(3)
            vgather(1)
            vbounce(4)
            vbounce(5)
            vgather(2)
            vbounce(6)
            vbounce(7)
            vgather(3)

        block.gpsimd(sec_gp)

        def sec_mm(t):
            jlist = list(range(8, 16)) + list(range(8))
            y_waited = set()

            def qk_idx(idx, ths=(0, 1)):
                j = jlist[idx]
                par = idx % 2
                W(t, QT[par], ("qgH", j))
                for otl in (0, 1):
                    for th in ths:
                        b = 4 * par + otl * 2 + th
                        first_scan = False
                        if th not in y_waited:
                            if th == 0:
                                first_scan = True
                            else:
                                W(t, DV, ("y", ytag, th))
                            y_waited.add(th)
                        if idx < 2:
                            use_bank(t, b)
                        elif idx in (8, 9):
                            W(t, DV, ("ve", idx - 8, 2 * otl + th))
                        else:
                            W(t, AC, ("qe", jlist[idx - 2], otl, th))
                        ins = None
                        for c in range(CD):
                            if c == 8:
                                W(t, QS[par], ("qg", j))
                            if first_scan:
                                W(t, DV if c % 2 == 0 else GP,
                                  ("yac", ytag, c))
                            ins = t.matmul(
                                pb[b][:, :],
                                slabW[:, 8192 + par * 4096 + c * 256 + otl * 128:
                                      8192 + par * 4096 + c * 256 + otl * 128 + 128],
                                y_bf[:, c * T + 512 * th:c * T + 512 * th + 512],
                                start=(c == 0), stop=(c == CD - 1))
                        PE.inc(ins)
                        R(("qkmm", j, otl, th), PE.n)

            def v_u(u):
                d, ttg = divmod(u, 2)
                wpar = d % 2
                if ttg == 0:
                    W(t, VG if d == 1 else KB, ("vg", d))
                par = u % 2
                for c in range(CD):
                    for tti in range(4):
                        tt = ttg * 4 + tti
                        b = 4 * par + tti
                        if c == 0:
                            if u < 2:
                                W(t, AC, ("qe", jlist[6 + u], tti >> 1, tti & 1))
                            elif u < 4:
                                W(t, AC, ("qe", jlist[12 + u], tti >> 1, tti & 1))
                            else:
                                W(t, DV, ("ve", u - 2, tti))
                        if d == 1:
                            wsrc = slabA[:, 24576 + c * 512:24576 + c * 512 + 512]
                        else:
                            wsrc = slabW[:, wpar * 8192 + c * 512:
                                         wpar * 8192 + c * 512 + 512]
                        ins = t.matmul(
                            pb[b][:, :],
                            y_bf[:, c * T + tt * 128:c * T + tt * 128 + 128],
                            wsrc,
                            start=(c == 0), stop=(c == CD - 1))
                        if c == CD - 1:
                            PE.inc(ins)
                            R(("vmm", u, tti), PE.n)

            qk_idx(0, (0,))
            qk_idx(1, (0,))
            qk_idx(0, (1,))
            qk_idx(1, (1,))
            for idx in range(2, 8):
                qk_idx(idx)
            v_u(0)
            v_u(1)
            for idx in range(8, 16):
                qk_idx(idx)
            for u in range(2, 8):
                v_u(u)
            R("qkv_pe_end", PE.n)

        block.tensor(sec_mm)

        def sec_qe(s):
            jlist = list(range(8, 16)) + list(range(8))
            order = ([(i, otl, 0) for i in (0, 1) for otl in (0, 1)]
                     + [(i, otl, 1) for i in (0, 1) for otl in (0, 1)]
                     + [(i, otl, th) for i in range(2, 16)
                        for otl in (0, 1) for th in (0, 1)])
            for idx, otl, th in order:
                j = jlist[idx]
                par = idx % 2
                ot = j * 2 + otl
                b = 4 * par + otl * 2 + th
                W(s, PE, ("qkmm", j, otl, th))
                AC.inc(s.activation(
                    slabA[:, ot * T + 512 * th:ot * T + 512 * th + 512],
                    pb[b][:, :], AF.Copy))
                R(("qe", j, otl, th), AC.n)

        block.scalar(sec_qe)

        def sec_vec(v):
            vstage = ln_scr[:, :].bitcast(BF16)
            for d in range(4):
                for ttg in (0, 1):
                    u = d * 2 + ttg
                    par = u % 2
                    for tti in range(4):
                        b = 4 * par + tti
                        W(v, PE, ("vmm", u, tti))
                        if u >= 1:
                            W(v, VI[(u - 1) // 2], ("vin", u - 1))
                        DV.inc(v.tensor_copy(
                            vstage[:, tti * 512:tti * 512 + 512],
                            pb[b][:, :]))
                        R(("ve", u, tti), DV.n)

        block.vector(sec_vec)

        def sec_sy(sy):
            for g2 in range(2):
                W(sy, AC, ("qe", 11 if g2 == 0 else 15, 1, 1))
                src = slabA[:, 16384 + g2 * 8192:16384 + (g2 + 1) * 8192
                            ].rearrange("p (ko t) -> p ko t", ko=8)
                dst = k_ins[g2].ap()[:, :].rearrange(
                    "(ko p) t -> p ko t", p=128)
                (KN if g2 == 0 else KM).inc(sy.dma_start(out=dst, in_=src), 16)
                R(("kin", g2), (KN if g2 == 0 else KM).n)
            vstage = ln_scr[:, :].bitcast(BF16)
            for u in range(2):
                d, ttg = divmod(u, 2)
                W(sy, DV, ("ve", u, 3))
                if u == 1:
                    W(sy, VI[d], ("vin", 0))
                dst = v_ins[d].ap()[ttg * 512:(ttg + 1) * 512, :].rearrange(
                    "(tt p) d2 -> p tt d2", p=128)
                vsrc = vstage[:, 0:2048].rearrange("p (tt d2) -> p tt d2", tt=4)
                VI[d].inc(sy.dma_start(out=dst, in_=vsrc), 16)
                R(("vin", u), VI[d].n)

        block.sync(sec_sy)
        for par in (0, 1):
            for tti in range(4):
                bank_free[4 * par + tti] = (DV, ("ve", 6 + par, tti))

    def emit_attn():
        NKT = S // 128

        def sec_sy(sy):
            cc_k = 0
            for h in range(NH):
                for half in (0, 1):
                    n = 2 * h + half
                    slot = n % 2
                    need = 1 if h < 8 else 3
                    if need > cc_k:
                        if real:
                            sy.wait_ge(CC.h, need)
                        cc_k = need
                    if n >= 2:
                        ph, phalf = divmod(n - 2, 2)
                        W(sy, PE, ("asc", ph, phalf * 32 + 31, 1))
                        W(sy, KR[slot], ("aKH", ph, phalf))
                    ksrc = k_outs[h // 8].ap().rearrange(
                        "(r d) t -> r d t", d=1024)[
                        half * 4:half * 4 + 4,
                        (h % 8) * 128:(h % 8) * 128 + 128, :
                    ].transpose([1, 0, 2])
                    kdst = slabA[:, KOFF + slot * 4096:KOFF + (slot + 1) * 4096
                                 ].rearrange("p (r t) -> p r t", r=4)
                    KR[slot].inc(sy.dma_start(out=kdst, in_=ksrc), 16)
                    R(("aKH", h, half), KR[slot].n)
                    if h % 4 == 0 and half == 0:
                        if real:
                            sy.wait_ge(CC.h, (2, 4, 5, 6)[h // 4])
                    if n >= 3:
                        ph, phalf = divmod(n - 3, 2)
                        W(sy, PE, ("apv", ph, phalf * 32 + 31, 1))
                        W(sy, VR[n % 3], ("aVH", ph, phalf))
                    vslot3 = n % 3
                    vot = v_outs[h // 4]
                    for rl in range(4):
                        r = half * 4 + rl
                        vsrc = vot.ap()[r * T:(r + 1) * T,
                                        (h % 4) * 128:(h % 4) * 128 + 128
                                        ].rearrange("(k p) d -> p k d", p=128)
                        vdst = slabW[:, vslot3 * 4096 + rl * 1024:
                                     vslot3 * 4096 + rl * 1024 + 1024
                                     ].rearrange("p (k d) -> p k d", k=8)
                        VR[vslot3].inc(sy.dma_start(out=vdst, in_=vsrc), 16)
                    R(("aVH", h, half), VR[vslot3].n)

        block.sync(sec_sy)

        events = []
        for h in range(NH):
            for kt in range(NKT):
                events.append(("sc", h, kt))
                if h > 0 and kt == 3:
                    events.append(("den", h - 1))
                if h > 0 and kt == 16:
                    events.append(("rb", h - 1))
        events.append(("den", NH - 1))
        events.append(("rb", NH - 1))
        slot_of = []
        rel_ev = []
        prev_user = [None, None, None]
        for i, ev in enumerate(events):
            s = i % 3
            slot_of.append(s)
            rel_ev.append(prev_user[s])
            prev_user[s] = i
        sc_slot = {}
        den_slot = {}
        rb_slot = {}
        for i, ev in enumerate(events):
            if ev[0] == "sc":
                sc_slot[(ev[1], ev[2])] = slot_of[i]
            elif ev[0] == "den":
                den_slot[ev[1]] = slot_of[i]
            else:
                rb_slot[ev[1]] = slot_of[i]

        bcf = slabA[:, :].bitcast(F32)

        def pacc_a(qh):
            return slabA[:, PAOFF + qh * 512:PAOFF + qh * 512 + 512]

        def pacc_b(qh):
            return bcf[:, 15360 + qh * 512:15360 + qh * 512 + 512]

        def pacc_bh(qh):
            return slabA[:, 29696 + qh * 512:29696 + qh * 512 + 512]

        def sec_mm(t):
            W(t, PE, "qkv_pe_end")

            def rel_wait(i):
                j = rel_ev[i]
                if j is None:
                    s = slot_of[i]
                    use_bank(t, 2 * s)
                    use_bank(t, 2 * s + 1)
                    return
                pe_ = events[j]
                if pe_[0] == "sc":
                    W(t, AC, ("axp", pe_[1], pe_[2]))
                elif pe_[0] == "den":
                    W(t, AC, ("aln", pe_[1]))
                else:
                    W(t, DV, ("aao", pe_[1]))

            def pv_mm(h, kt):
                half = kt // 32
                vslot = (2 * h + half) % 3
                W(t, AC, ("axp", h, kt))
                if kt % 32 == 0:
                    W(t, VR[vslot], ("aVH", h, half))
                vcol = (vslot * 4096 + ((kt // 8) % 4) * 1024
                        + (kt % 8) * 128)
                for qh in (0, 1):
                    if kt == 0:
                        if h == 0:
                            use_bank(t, 6 + qh)
                        else:
                            W(t, DV, ("apc", h - 1))
                    PE.inc(t.matmul(
                        pb[6 + qh][:, :],
                        slabW[:, vcol:vcol + 128],
                        slabA[:, POFF + (kt % 4) * 1024 + qh * 512:
                              POFF + (kt % 4) * 1024 + qh * 512 + 512],
                        start=(kt == 0), stop=(kt == NKT - 1)))
                    R(("apv", h, kt, qh), PE.n)

            sc_lin = []
            for i, ev in enumerate(events):
                if ev[0] == "sc":
                    _, h, kt = ev
                    s = slot_of[i]
                    if kt == 0:
                        W(t, AC, ("qe", h // 2, h % 2, 1))
                    half = kt // 32
                    kslot = (2 * h + half) % 2
                    if kt % 32 == 0:
                        W(t, KR[kslot], ("aKH", h, half))
                    kcol = (KOFF + kslot * 4096 + ((kt // 8) % 4) * 1024
                            + (kt % 8) * 128)
                    rel_wait(i)
                    for qh in (0, 1):
                        PE.inc(t.matmul(
                            pp[s][:, qh * 512:qh * 512 + 512],
                            slabA[:, kcol:kcol + 128],
                            slabA[:, h * T + qh * 512:h * T + qh * 512 + 512],
                            start=True, stop=True))
                        R(("asc", h, kt, qh), PE.n)
                    sc_lin.append((h, kt))
                    if len(sc_lin) >= 5:
                        pv_mm(*sc_lin[-5])
                elif ev[0] == "den":
                    hh = ev[1]
                    if hh == NH - 1:
                        for tr in sc_lin[-4:]:
                            pv_mm(*tr)
                    s = slot_of[i]
                    rel_wait(i)
                    W(t, DV, ("apbh", hh))
                    for qh in (0, 1):
                        PE.inc(t.matmul(
                            pp[s][0:1, qh * 512:qh * 512 + 512],
                            ones_b[:, 0:1], pacc_bh(qh),
                            start=True, stop=True))
                    R(("aden2", hh), PE.n)
                else:
                    hh = ev[1]
                    s = slot_of[i]
                    rel_wait(i)
                    W(t, AC, ("arecip", hh))
                    sbf = stats[0:1, :].bitcast(BF16)
                    for qh in (0, 1):
                        PE.inc(t.matmul(
                            pp[s][:, qh * 512:qh * 512 + 512],
                            onesk1b[0:1, :],
                            sbf[0:1, 2048 + qh * 512:2560 + qh * 512],
                            start=True, stop=True))
                    R(("arb", hh), PE.n)
            R("attn_pe_end", PE.n)

        block.tensor(sec_mm)

        def sec_act(s):
            def act_tail(hh):
                W(s, PE, ("aden2", hh))
                ds = den_slot[hh]
                sbf = stats[0:1, :].bitcast(BF16)
                for qh in (0, 1):
                    AC.inc(s.activation(stats[0:1, qh * 512:qh * 512 + 512],
                                        pp[ds][0:1, qh * 512:qh * 512 + 512],
                                        AF.Ln, bias=cst[0:1, 1:2]))
                R(("aln", hh), AC.n)
                for qh in (0, 1):
                    if real:
                        s.wait_ge(AC.h, AC.n)
                    AC.inc(s.activation(
                        sbf[0:1, 2048 + qh * 512:2560 + qh * 512],
                        stats[0:1, qh * 512:qh * 512 + 512],
                        AF.Exp, scale=-1.0))
                R(("arecip", hh), AC.n)

            for h in range(NH):
                for kt in range(NKT):
                    W(s, PE, ("asc", h, kt, 1))
                    pk = h * NKT + kt
                    if pk >= 4:
                        pph, ppkt = divmod(pk - 4, NKT)
                        W(s, PE, ("apv", pph, ppkt, 1))
                        W(s, DV, ("apa", pph, ppkt))
                    AC.inc(s.activation(
                        slabA[:, POFF + (kt % 4) * 1024:
                              POFF + (kt % 4) * 1024 + 1024],
                        pp[sc_slot[(h, kt)]][:, :], AF.Exp,
                        scale=float(SCALE)))
                    R(("axp", h, kt), AC.n)
                    if kt == 4 and h > 0:
                        act_tail(h - 1)
            act_tail(NH - 1)

        block.scalar(sec_act)

        def sec_vec(v):
            def aao(hh):
                W(v, PE, ("arb", hh))
                rs = rb_slot[hh]
                for qh in (0, 1):
                    if real:
                        v.wait_ge(DV.h, DV.n)
                    DV.inc(v.tensor_tensor(
                        y_bf[:, hh * T + qh * 512:hh * T + qh * 512 + 512],
                        y_bf[:, hh * T + qh * 512:hh * T + qh * 512 + 512],
                        pp[rs][:, qh * 512:qh * 512 + 512], ALU.mult))
                R(("aao", hh), DV.n)

            def apc(hh):
                if hh == 0:
                    W(v, PE, "qkv_pe_end")
                for qh in (0, 1):
                    W(v, PE, ("apv", hh, NKT - 1, qh))
                    if real:
                        v.wait_ge(DV.h, DV.n)
                    DV.inc(v.tensor_copy(
                        y_bf[:, hh * T + qh * 512:hh * T + qh * 512 + 512],
                        pb[6 + qh][:, :]))
                R(("apc", hh), DV.n)

            paccA = slabA[:, PAOFF:PAOFF + 1024]
            paccB = bcf[:, 15360:16384]
            paccBH = slabA[:, 29696:30720]
            for h in range(NH):
                for kt in range(NKT):
                    g = kt // 8
                    W(v, AC, ("axp", h, kt))
                    if real:
                        v.wait_ge(DV.h, DV.n)
                    psrc = slabA[:, POFF + (kt % 4) * 1024:
                                 POFF + (kt % 4) * 1024 + 1024]
                    if kt % 8 == 0:
                        DV.inc(v.tensor_copy(paccA, psrc))
                    else:
                        DV.inc(v.tensor_add(paccA, paccA, psrc))
                    R(("apa", h, kt), DV.n)
                    if kt % 8 == 7:
                        if real:
                            v.wait_ge(DV.h, DV.n)
                        if g == 0:
                            DV.inc(v.tensor_copy(paccB, paccA))
                        else:
                            DV.inc(v.tensor_add(paccB, paccB, paccA))
                        R(("apb", h, g), DV.n)
                        if g == 7:
                            if real:
                                v.wait_ge(DV.h, DV.n)
                            DV.inc(v.tensor_copy(paccBH, paccB))
                            R(("apbh", h), DV.n)
                    if h > 0 and kt == 1:
                        apc(h - 1)
                    if h > 0 and kt == 13:
                        aao(h - 1)
            apc(NH - 1)
            aao(NH - 1)

        block.vector(sec_vec)
        for s in range(3):
            i = prev_user[s]
            ev = events[i]
            if ev[0] == "sc":
                rel = (AC, ("axp", ev[1], ev[2]))
            elif ev[0] == "den":
                rel = (AC, ("aln", ev[1]))
            else:
                rel = (DV, ("aao", ev[1]))
            bank_free[2 * s] = rel
            bank_free[2 * s + 1] = rel
        bank_free[6] = (DV, ("apc", NH - 1))
        bank_free[7] = (DV, ("apc", NH - 1))

    def emit_oproj():
        jlist2 = list(range(8, 16)) + list(range(8))

        def sec_gp(g):
            for j in range(8):
                if j == 0:
                    W(g, PE, ("vmm", 7, 3))
                elif j == 1:
                    W(g, PE, ("apv", 14, 63, 1))
                else:
                    W(g, PE, ("omm", j - 2, 1, 1))
                par = j % 2
                if j < 2:
                    W(g, QS[par], ("qg", jlist2[14 + j]))
                else:
                    W(g, QS[par], ("og", j - 2))
                ob = 12288 if par == 0 else 8192
                for cg in range(4):
                    src = woutT_d.ap()[cg * 512:(cg + 1) * 512,
                                       j * 256:(j + 1) * 256].rearrange(
                        "(c p) o -> p c o", p=128)
                    dst = slabW[:, ob + cg * 1024:ob + cg * 1024 + 1024
                                ].rearrange("p (c o) -> p c o", c=4)
                    QS[par].inc(g.dma_start(out=dst, in_=src), 16)
                R(("og", j), QS[par].n)
                if j == 1:
                    for jj in (0, 1):
                        W(g, PE, ("apv", NH - 1, 31 if jj == 0 else 63, 1))
                        gpar = jj % 2
                        ring_guard(g, WS[gpar], ("ws", gpar),
                                   ("fgG", "f2", 0, jj))
                        for cg in range(4):
                            src = wg_d.ap()[cg * 512:(cg + 1) * 512,
                                            jj * 256:(jj + 1) * 256].rearrange(
                                "(c p) o -> p c o", p=128)
                            dst = slabW[:, gpar * 4096 + cg * 1024:
                                        gpar * 4096 + cg * 1024 + 1024
                                        ].rearrange("p (c o) -> p c o", c=4)
                            WS[gpar].inc(g.dma_start(out=dst, in_=src), 16)
                        R(("fgG", "f2", 0, jj), WS[gpar].n)

        block.gpsimd(sec_gp)

        def sec_mm(t):
            W(t, DV, ("aao", NH - 1))
            for j in range(8):
                par = j % 2
                W(t, QS[par], ("og", j))
                for otl in (0, 1):
                    for th in (0, 1):
                        b = 4 * par + otl * 2 + th
                        if j < 2:
                            use_bank(t, b)
                        else:
                            W(t, DV, ("oe", j - 2, otl, th))
                    ob = 12288 if par == 0 else 8192
                    for c in range(CD):
                        for th in (0, 1):
                            b = 4 * par + otl * 2 + th
                            ins = t.matmul(
                                pb[b][:, :],
                                slabW[:, ob + c * 256 + otl * 128:
                                      ob + c * 256 + otl * 128 + 128],
                                y_bf[:, c * T + 512 * th:c * T + 512 * th + 512],
                                start=(c == 0), stop=(c == CD - 1))
                            if c == CD - 1:
                                PE.inc(ins)
                                R(("omm", j, otl, th), PE.n)
            R("oproj_pe_end", PE.n)

        block.tensor(sec_mm)

        def sec_vec(v):
            for j in range(8):
                par = j % 2
                for otl in (0, 1):
                    c = j * 2 + otl
                    for th in (0, 1):
                        b = 4 * par + otl * 2 + th
                        W(v, PE, ("omm", j, otl, th))
                        DV.inc(v.tensor_tensor(
                            x_res[:, c * T + 512 * th:c * T + 512 * th + 512],
                            pb[b][:, :],
                            x_res[:, c * T + 512 * th:c * T + 512 * th + 512],
                            ALU.add))
                        R(("oe", j, otl, th), DV.n)

        block.vector(sec_vec)
        for par in (0, 1):
            for otl in (0, 1):
                for th in (0, 1):
                    bank_free[4 * par + otl * 2 + th] = (DV, ("oe", 6 + par, otl, th))

    def emit_store(x_ready):
        def store_c(eng, c, sem):
            sv = x_ready(c)
            if sv is not None:
                W(eng, sv[0], sv[1])
            sem.inc(eng.dma_start(
                out=out_d.ap()[c * 128:(c + 1) * 128, :],
                in_=x_res[:, c * T:(c + 1) * T]), 16)

        def sec_sy(sy):
            for c in range(0, CD, 2):
                store_c(sy, c, SX)
            R("store_sx", SX.n)
            if real:
                sy.wait_ge(SX.h, rec["store_sx"])
                sy.wait_ge(GX.h, rec["store_gx"])

        def sec_gp(g):
            for c in range(1, CD, 2):
                store_c(g, c, GX)
            R("store_gx", GX.n)
            if real:
                g.wait_ge(SX.h, rec["store_sx"])
                g.wait_ge(GX.h, rec["store_gx"])

        block.sync(sec_sy)
        block.gpsimd(sec_gp)

    gp1 = emit_ln("ln1", lambda c: (SX if c < 8 else GX, ("xg", c // 8)), None)
    emit_ffn("f1", "ln1", None, gp_pre=gp1)
    if _BUILD_UPTO >= 2:
        gp2 = emit_ln("ln2", lambda c: (DV, ("ffnx", "f1", 1, c // 8, c % 8)),
                      (PE, ("ffny", "f1")))
        emit_qkv("ln2", None, gp_pre=gp2)
    if _BUILD_UPTO >= 3:
        emit_attn()
        emit_oproj()
    if _BUILD_UPTO >= 4:
        gp3 = emit_ln("ln3", lambda c: (DV, ("oe", c // 2, c % 2, 1)),
                      (PE, "oproj_pe_end"))
        emit_ffn("f2", "ln3", None, skip_first_gates=True, gp_pre=gp3)

    if _BUILD_UPTO >= 4:
        emit_store(lambda c: (DV, ("ffnx", "f2", 1, c // 8, c % 8)))
    elif _BUILD_UPTO == 3:
        emit_store(lambda c: (DV, ("oe", c // 2, c % 2, 1)))
    else:
        emit_store(lambda c: (DV, ("ffnx", "f1", 1, c // 8, c % 8)))

    if _BUILD_UPTO == 2:
        def sec_tail(g):
            if real:
                g.wait_ge(CC.h, 5)
        block.gpsimd(sec_tail)

    return {i: s.n for i, s in enumerate(sems)}


def build_program():
    nc = bass.Bass("TRN2", target_bir_lowering=False)

    x_d = nc.dram_tensor("x", [DM, T], F32, kind="ExternalInput")
    wqkvT_d = nc.dram_tensor("wqkvT", [DM, 3 * DM], F32, kind="ExternalInput")
    woutT_d = nc.dram_tensor("woutT", [DM, DM], F32, kind="ExternalInput")
    wg_d = nc.dram_tensor("wg", [DM, IN], I8, kind="ExternalInput")
    wu_d = nc.dram_tensor("wu", [DM, IN], I8, kind="ExternalInput")
    wd_d = nc.dram_tensor("wd", [IN, DM], I8, kind="ExternalInput")
    sg_d = nc.dram_tensor("sg", [128, NI], F32, kind="ExternalInput")
    su_d = nc.dram_tensor("su", [128, NI], F32, kind="ExternalInput")
    sd_d = nc.dram_tensor("sd", [128, CD], F32, kind="ExternalInput")
    out_d = nc.dram_tensor("out", [DM, T], F32, kind="ExternalOutput")

    k_ins = [nc.dram_tensor(f"k_in{i}", [1024, T], BF16) for i in range(2)]
    k_outs = [nc.dram_tensor(f"k_out{i}", [NCORES * 1024, T], BF16,
                             addr_space="Shared") for i in range(2)]
    v_ins = [nc.dram_tensor(f"v_in{i}", [T, 512], BF16) for i in range(4)]
    v_outs = [nc.dram_tensor(f"v_out{i}", [NCORES * T, 512], BF16,
                             addr_space="Shared") for i in range(4)]

    with ExitStack() as stk:
        e = stk.enter_context
        x_res = e(nc.sbuf_tensor("x_res", [128, CD * T], F32))
        y_bf = e(nc.sbuf_tensor("y_bf", [128, CD * T], BF16))
        slabA = e(nc.sbuf_tensor("slabA", [128, 32768], BF16))
        slabW = e(nc.sbuf_tensor("slabW", [128, 16384], BF16))
        ln_scr = e(nc.sbuf_tensor("ln_scr", [128, 1024], F32))
        stats = e(nc.sbuf_tensor("stats", [1, 2048], F32))
        gtmp = e(nc.sbuf_tensor("gtmp", [128, 1024], BF16))
        sg_s = e(nc.sbuf_tensor("sg_s", [128, NI], F32))
        su_s = e(nc.sbuf_tensor("su_s", [128, NI], F32))
        sd_s = e(nc.sbuf_tensor("sd_s", [128, CD], F32))
        sdh_s = e(nc.sbuf_tensor("sdh_s", [128, CD], F32))
        ones_b = e(nc.sbuf_tensor("ones_b", [128, 1], BF16))
        ones_f = e(nc.sbuf_tensor("ones_f", [128, 1], F32))
        onesk1 = e(nc.sbuf_tensor("onesk1", [1, 128], F32))
        onesk1b = e(nc.sbuf_tensor("onesk1b", [1, 128], BF16))
        cst = e(nc.sbuf_tensor("cst", [1, 2], F32))
        pp = [e(nc.psum_tensor(f"pp{i}", [128, 1024], F32)) for i in range(4)]
        pb = [pp[i // 2][:, (i % 2) * 512:(i % 2) * 512 + 512] for i in range(8)]

        tensors = (x_res, y_bf, slabA, slabW, ln_scr, stats, gtmp, sg_s, su_s,
                   sd_s, sdh_s, ones_b, ones_f, onesk1, onesk1b, cst, pb, pp,
                   x_d, wqkvT_d, woutT_d, wg_d, wu_d, wd_d, sg_d, su_d, sd_d,
                   out_d, k_ins, k_outs, v_ins, v_outs)

        rec = {}
        SEM_NAMES = ["SX", "SC", "KB", "VA", "VB", "VC", "VD",
                     "WA", "WB", "QA", "QB", "DA", "DB", "DC", "DD",
                     "KR0", "KR1", "KR2", "VR0", "VR1", "VR2",
                     "CC", "PE", "AC", "DV", "GP", "KN", "GX", "QT0", "QT1",
                     "UA", "UB", "KM", "VG"]
        plan_sems = tuple(PlanSem() for _ in SEM_NAMES)
        plan_counts = emit_all(_FakeBlock(), nc, tensors, plan_sems, rec,
                               real=False)

        real_sems = tuple(Sem(stk, nc, n) for n in SEM_NAMES)
        block = e(nc.Block())
        real_counts = emit_all(block, nc, tensors, real_sems, rec, real=True)
        assert plan_counts == real_counts, (plan_counts, real_counts)

    return nc


_NC_CACHE = None


def _enable_ldw_opt():
    try:
        import libneuronxla.libncc as ncc
        for i, f in enumerate(ncc.NEURON_CC_FLAGS):
            if f.startswith('--internal-backend-options='):
                ncc.NEURON_CC_FLAGS[i] = f.replace(
                    '--enable-ldw-opt=false', '--enable-ldw-opt=true')
    except Exception:
        pass


def _get_program():
    global _NC_CACHE
    if _NC_CACHE is None:
        _enable_ldw_opt()
        _NC_CACHE = build_program()
    return _NC_CACHE


def kernel(L0_attn_x, L0_attn_w_qkv, L0_attn_w_out, L0_ffn_w_gate,
           L0_ffn_w_up, L0_ffn_w_down, L0_ffn_s_gate, L0_ffn_s_up,
           L0_ffn_s_down):
    x = np.asarray(L0_attn_x)
    wqkvT = np.ascontiguousarray(np.asarray(L0_attn_w_qkv).T)
    woutT = np.ascontiguousarray(np.asarray(L0_attn_w_out).T)
    wg = np.ascontiguousarray(np.asarray(L0_ffn_w_gate))
    wu = np.ascontiguousarray(np.asarray(L0_ffn_w_up))
    wd = np.ascontiguousarray(np.asarray(L0_ffn_w_down))
    sg = np.ascontiguousarray(np.asarray(L0_ffn_s_gate).reshape(NI, 128).T)
    su = np.ascontiguousarray(np.asarray(L0_ffn_s_up).reshape(NI, 128).T)
    sd = np.ascontiguousarray(np.asarray(L0_ffn_s_down).reshape(CD, 128).T)

    nc = _get_program()
    in_maps = []
    for c in range(NCORES):
        xc = np.ascontiguousarray(x[0, c * T:(c + 1) * T, :].T)
        in_maps.append({"x": xc, "wqkvT": wqkvT, "woutT": woutT,
                        "wg": wg, "wu": wu, "wd": wd,
                        "sg": sg, "su": su, "sd": sd})
    res = run_bass_kernel_spmd(nc, in_maps, core_ids=list(range(NCORES)))
    out = np.concatenate([np.asarray(r["out"]).T for r in res.results], axis=0)
    return np.ascontiguousarray(out).reshape(1, S, DM)


# version 42 (final)
# speedup vs baseline: 1.0008x; 1.0008x over previous
import numpy as np
import concourse.bass as bass
import concourse.mybir as mybir
from concourse.bass_utils import run_bass_kernel_spmd
from contextlib import ExitStack

F32 = mybir.dt.float32
BF16 = mybir.dt.bfloat16
I8 = mybir.dt.int8
AF = mybir.ActivationFunctionType
ALU = mybir.AluOpType

NCORES = 8
T = 1024
DM = 2048
CD = DM // 128
IN = 8192
NI = IN // 128
NOG = IN // 256
NH = 16
S = NCORES * T
SCALE = 1.0 / float(np.sqrt(128.0))
LN_EPS = 1e-5

_BUILD_UPTO = 4

KOFF = 16384
POFF = 24576
PAOFF = 28672


class PlanSem:
    __slots__ = ("h", "n")

    def __init__(self):
        self.h = None
        self.n = 0

    def inc(self, inst, k=1):
        self.n += k
        return self.n


class Sem:
    def __init__(self, stack, nc, name):
        self.h = stack.enter_context(nc.semaphore(name))
        self.n = 0

    def inc(self, inst, k=1):
        inst.then_inc(self.h, k)
        self.n += k
        return self.n


class _FakeInst:
    def then_inc(self, *a, **k):
        return self


class _FakeEng:
    def __getattr__(self, name):
        def f(*a, **k):
            return _FakeInst()
        return f


class _FakeBlock:
    def __getattr__(self, name):
        def runner(f):
            f(_FakeEng())
        return runner


def emit_all(block, nc, tensors, sems, rec, real):
    (x_res, y_bf, slabA, slabW, ln_scr, stats, gtmp, sg_s, su_s, sd_s,
     sdh_s, ones_b, ones_f, onesk1, onesk1b, cst, pb, pp,
     x_d, wqkvT_d, woutT_d, wg_d, wu_d, wd_d, sg_d, su_d, sd_d, out_d,
     k_ins, k_outs, v_ins, v_outs) = tensors
    (SX, SC, KB, VA, VB, VC, VD, WA, WB, QA, QB, DA, DB, DC, DD,
     KR0, KR1, KR2, VR0, VR1, VR2, CC, PE, AC, DV, GP, KN, GX,
     QT0, QT1, UA, UB, KM, VG) = sems
    QT = [QT0, QT1]
    US = [UA, UB]
    VI = [VA, VB, VC, VD]
    KR = [KR0, KR1, KR2]
    VR = [VR0, VR1, VR2]
    WS = [WA, WB]
    QS = [QA, QB]
    DS = [DA, DB, DC, DD]

    def W(eng, sem, key):
        if real:
            eng.wait_ge(sem.h, rec[key])

    def R(key, val):
        if real:
            assert rec[key] == val, (key, rec[key], val)
        rec[key] = val

    bank_free = {}
    ring_last = {}

    def ring_guard(eng, sem, name, key):
        prev = ring_last.get(name)
        if prev is not None:
            W(eng, sem, prev)
        ring_last[name] = key

    def use_bank(eng, i):
        if i in bank_free and bank_free[i] is not None:
            sem, key = bank_free[i]
            W(eng, sem, key)
            bank_free[i] = None

    def xload(eng, g, sem):
        src = x_d.ap()[g * 512:(g + 1) * 512, :].rearrange(
            "(c p) t -> p c t", p=128)
        dst = x_res[:, g * 4 * T:(g + 1) * 4 * T].rearrange(
            "p (c t) -> p c t", c=4)
        sem.inc(eng.dma_start(out=dst, in_=src), 16)
        if g % 2 == 1:
            R(("xg", g // 2), sem.n)

    def sec_init_sync(sy):
        xload(sy, 0, SX)
        xload(sy, 1, SX)
        SC.inc(sy.dma_start(out=sg_s[:], in_=sg_d.ap()), 16)
        SC.inc(sy.dma_start(out=su_s[:], in_=su_d.ap()), 16)
        SC.inc(sy.dma_start(out=sd_s[:], in_=sd_d.ap()), 16)
        R("scales", SC.n)

    block.sync(sec_init_sync)

    def sec_init_gp(g):
        xload(g, 2, GX)
        xload(g, 3, GX)

    block.gpsimd(sec_init_gp)

    def sec_init_vec(v):
        v.memset(ones_b[:], 1.0)
        v.memset(ones_f[:], 1.0)
        v.memset(onesk1[:], 1.0)
        v.memset(onesk1b[:], 1.0)
        v.memset(cst[0:1, 0:1], LN_EPS)
        v.memset(cst[0:1, 1:2], 1e-8)
        W(v, SC, "scales")
        DV.inc(v.tensor_scalar(sdh_s[:], sd_s[:], 0.5, None, ALU.mult))
        R("init_dv", DV.n)

    block.vector(sec_init_vec)

    def emit_ln(tag, x_ready, y_free):
        bcf = slabA[:, :].bitcast(F32)

        def bcA(th):
            return bcf[:, 14336 + th * 512:14336 + th * 512 + 512]

        def bcB(th):
            return bcf[:, 15360 + th * 512:15360 + th * 512 + 512]
        def sec_sq(s):
            for c in range(CD):
                if x_ready is not None:
                    sv = x_ready(c)
                    if sv is not None:
                        W(s, sv[0], sv[1])
                for th in (0, 1):
                    if c >= 1:
                        W(s, PE, ("lnst2", tag, c - 1, th))
                    AC.inc(s.activation(
                        gtmp[:, 512 * th:512 * th + 512],
                        x_res[:, c * T + 512 * th:c * T + 512 * th + 512],
                        AF.Square))
                    R(("lnsq2", tag, c, th), AC.n)

        def sec_cp(v):
            xbf = ln_scr[:, :].bitcast(BF16)
            for c in range(CD):
                if x_ready is not None:
                    sv = x_ready(c)
                    if sv is not None:
                        W(v, sv[0], sv[1])
                for th in (0, 1):
                    sl = (c % 2) * 2 + th
                    if c >= 2:
                        W(v, PE, ("lnst2", tag, c - 2, th))
                    DV.inc(v.tensor_copy(
                        xbf[:, sl * 512:sl * 512 + 512],
                        x_res[:, c * T + 512 * th:c * T + 512 * th + 512]))
                    R(("lncp", tag, c, th), DV.n)

        def sec_rstd(s):
            for th in (0, 1):
                W(s, DV, ("lnvar", tag, th))
                if real:
                    s.wait_ge(AC.h, AC.n)
                AC.inc(s.activation(ln_scr[0:1, 512:1024],
                                    stats[0:1, 1024 + 512 * th:1536 + 512 * th],
                                    AF.Ln, bias=cst[0:1, 0:1]))
                if real:
                    s.wait_ge(AC.h, AC.n)
                AC.inc(s.activation(
                    stats[0:1, :].bitcast(BF16)[0:1, 2048 + 512 * th:
                                                2560 + 512 * th],
                    ln_scr[0:1, 512:1024], AF.Exp, scale=-0.5))
                R(("lnrstd", tag, th), AC.n)
            for th in (0, 1):
                W(s, PE, ("lnbc", tag, th))
                AC.inc(s.activation(bcA(th), pb[4 + th][:, :], AF.Copy))
                AC.inc(s.activation(bcB(th), pb[6 + th][:, :], AF.Copy))
                R(("lncpy", tag, th), AC.n)

        block.vector(sec_cp)
        block.scalar(sec_sq)
        block.scalar(sec_rstd)

        def sec_mm(t):
            W(t, DV, "init_dv")
            for i in range(4):
                use_bank(t, i)
            xbf = ln_scr[:, :].bitcast(BF16)
            for c in range(CD):
                for th in (0, 1):
                    sl = (c % 2) * 2 + th
                    W(t, AC, ("lnsq2", tag, c, th))
                    W(t, DV, ("lncp", tag, c, th))
                    t.matmul(pb[th][0:1, :], ones_b[:, 0:1],
                             xbf[:, sl * 512:sl * 512 + 512],
                             start=(c == 0), stop=(c == CD - 1))
                    ins = t.matmul(pb[2 + th][0:1, :], ones_b[:, 0:1],
                                   gtmp[:, 512 * th:512 * th + 512],
                                   start=(c == 0), stop=(c == CD - 1))
                    PE.inc(ins)
                    R(("lnst2", tag, c, th), PE.n)
                R(("lnst", tag, c), PE.n)
            sbf = stats[0:1, :].bitcast(BF16)
            for th in (0, 1):
                use_bank(t, 4 + th)
                use_bank(t, 6 + th)
                W(t, DV, ("lnm", tag, th))
                t.matmul(pb[4 + th][:, :], onesk1b[0:1, :],
                         slabA[0:1, 27648 + 512 * th:27648 + 512 * th + 512],
                         start=True, stop=True)
                W(t, AC, ("lnrstd", tag, th))
                PE.inc(t.matmul(pb[6 + th][:, :], onesk1b[0:1, :],
                                sbf[0:1, 2048 + 512 * th:2560 + 512 * th],
                                start=True, stop=True))
                R(("lnbc", tag, th), PE.n)

        block.tensor(sec_mm)

        def sec_vec(v):
            mbf = slabA[0:1, :]
            W(v, PE, ("lnst", tag, CD - 1))
            for th in (0, 1):
                DV.inc(v.tensor_scalar(
                    mbf[0:1, 27648 + 512 * th:27648 + 512 * th + 512],
                    pb[th][0:1, :], 1.0 / DM, None, ALU.mult))
                R(("lnm", tag, th), DV.n)
            for th in (0, 1):
                DV.inc(v.tensor_scalar(stats[0:1, 1024 + 512 * th:1536 + 512 * th],
                                       pb[2 + th][0:1, :], 1.0 / DM, None,
                                       ALU.mult))
            R(("lnmsq", tag), DV.n)
            for th in (0, 1):
                if real:
                    v.wait_ge(DV.h, DV.n)
                DV.inc(v.tensor_tensor(
                    stats[0:1, 0:512],
                    mbf[0:1, 27648 + 512 * th:27648 + 512 * th + 512],
                    mbf[0:1, 27648 + 512 * th:27648 + 512 * th + 512],
                    ALU.mult))
                if real:
                    v.wait_ge(DV.h, DV.n)
                DV.inc(v.tensor_tensor(stats[0:1, 1024 + 512 * th:1536 + 512 * th],
                                       stats[0:1, 1024 + 512 * th:1536 + 512 * th],
                                       stats[0:1, 0:512], ALU.subtract))
                R(("lnvar", tag, th), DV.n)
            first = True
            for th in (0, 1):
                W(v, AC, ("lncpy", tag, th))
                for c in range(CD):
                    if th == 0 and c % 2 == 1:
                        continue
                    if first and y_free is not None:
                        W(v, y_free[0], y_free[1])
                    first = False
                    scr = ln_scr[:, (c % 2) * 512:(c % 2) * 512 + 512]
                    if real:
                        v.wait_ge(DV.h, DV.n)
                    DV.inc(v.tensor_tensor(
                        scr,
                        x_res[:, c * T + 512 * th:c * T + 512 * th + 512],
                        bcA(th), ALU.subtract))
                    if real:
                        v.wait_ge(DV.h, DV.n)
                    DV.inc(v.tensor_tensor(
                        y_bf[:, c * T + 512 * th:c * T + 512 * th + 512],
                        scr, bcB(th), ALU.mult))
                    if th == 0:
                        R(("yac", tag, c), DV.n)
                R(("y", tag, th), DV.n)

        block.vector(sec_vec)

        def gp_apply(g):
            if y_free is not None:
                W(g, y_free[0], y_free[1])
            W(g, AC, ("lncpy", tag, 0))
            gscr = gtmp[:, :].bitcast(F32)
            for c in range(1, CD, 2):
                if real:
                    g.wait_ge(GP.h, GP.n)
                GP.inc(g.tensor_tensor(
                    gscr[:, 0:512],
                    x_res[:, c * T:c * T + 512],
                    bcA(0), ALU.subtract))
                if real:
                    g.wait_ge(GP.h, GP.n)
                GP.inc(g.tensor_tensor(
                    y_bf[:, c * T:c * T + 512],
                    gscr[:, 0:512], bcB(0), ALU.mult))
                R(("yac", tag, c), GP.n)
            R(("yg0", tag), GP.n)
        for i in range(4):
            bank_free[i] = (DV, ("lnmsq", tag))
        for i in range(4, 8):
            bank_free[i] = (AC, ("lncpy", tag, 1))
        return gp_apply

    def emit_ffn(tag, ytag, w_free, skip_first_gates=False, gp_pre=None):
        def sec_gp(g):
            if w_free is not None:
                W(g, w_free[0], w_free[1])
            if skip_first_gates:
                for jj, key in ((1, ("omm", 6, 1, 1)), (0, ("omm", 7, 1, 1))):
                    W(g, PE, key)
                    par = jj % 2
                    ring_guard(g, US[par], ("us", par), ("fg", tag, 0, jj))
                    for cg in range(4):
                        src = wu_d.ap()[cg * 512:(cg + 1) * 512,
                                        jj * 256:(jj + 1) * 256].rearrange(
                            "(c p) o -> p c o", p=128)
                        dst = slabW[:, 8192 + par * 4096 + cg * 1024:
                                    8192 + par * 4096 + cg * 1024 + 1024
                                    ].rearrange("p (c o) -> p c o", c=4)
                        US[par].inc(g.dma_start(out=dst, in_=src), 16)
                    R(("fg", tag, 0, jj), US[par].n)
                if gp_pre is not None:
                    gp_pre(g)
            for th in (0, 1):
                for j in range(NOG):
                    lin = th * NOG + j
                    if skip_first_gates and lin < 2:
                        continue
                    par = j % 2
                    if th == 1 and j <= 1:
                        W(g, PE, ("fdmm", tag, 0, 1, 61 if j == 0 else 63))
                        ring_guard(g, WS[par], ("ws", par),
                                   ("fgG", tag, th, j))
                        for cg in range(4):
                            src = wg_d.ap()[cg * 512:(cg + 1) * 512,
                                            j * 256:(j + 1) * 256].rearrange(
                                "(c p) o -> p c o", p=128)
                            dst = slabW[:, par * 4096 + cg * 1024:
                                        par * 4096 + cg * 1024 + 1024
                                        ].rearrange("p (c o) -> p c o", c=4)
                            WS[par].inc(g.dma_start(out=dst, in_=src), 16)
                        R(("fgG", tag, th, j), WS[par].n)
                        continue
                    elif lin >= 2:
                        pth, pj = divmod(lin - 2, NOG)
                        W(g, PE, ("fup", tag, pth, pj, 1))
                    for w_dd, off in ((wg_d, 0), (wu_d, 8192)):
                        wsem = WS[par] if w_dd is wg_d else US[par]
                        if w_dd is wg_d:
                            ring_guard(g, wsem, ("ws", par),
                                       ("fgG", tag, th, j))
                        else:
                            ring_guard(g, wsem, ("us", par),
                                       ("fg", tag, th, j))
                        for cg in range(4):
                            src = w_dd.ap()[cg * 512:(cg + 1) * 512,
                                            j * 256:(j + 1) * 256].rearrange(
                                "(c p) o -> p c o", p=128)
                            dst = slabW[:, off + par * 4096 + cg * 1024:
                                        off + par * 4096 + cg * 1024 + 1024
                                        ].rearrange("p (c o) -> p c o", c=4)
                            wsem.inc(g.dma_start(out=dst, in_=src), 16)
                        if w_dd is wg_d:
                            R(("fgG", tag, th, j), wsem.n)
                    R(("fg", tag, th, j), US[par].n)
                    if gp_pre is not None and th == 0 and j == 1:
                        gp_pre(g)
                def up_half(jj):
                    W(g, PE, ("fup", tag, 0, NOG - 2 + jj, 1))
                    par = jj % 2
                    ring_guard(g, US[par], ("us", par), ("fg", tag, 1, jj))
                    for cg in range(4):
                        src = wu_d.ap()[cg * 512:(cg + 1) * 512,
                                        jj * 256:(jj + 1) * 256].rearrange(
                            "(c p) o -> p c o", p=128)
                        dst = slabW[:, 8192 + par * 4096 + cg * 1024:
                                    8192 + par * 4096 + cg * 1024 + 1024
                                    ].rearrange("p (c o) -> p c o", c=4)
                        US[par].inc(g.dma_start(out=dst, in_=src), 16)
                    R(("fg", tag, 1, jj), US[par].n)
                for oh in (0, 1):
                    for i in range(NI):
                        if th == 0 and oh == 0 and i in (2, 4):
                            up_half(0 if i == 2 else 1)
                        dk = th * 128 + oh * 64 + i
                        if oh == 0 and i < 4:
                            W(g, PE, ("fgate", tag, th,
                                      NOG - 2 if i < 2 else NOG - 1, 1))
                        if dk >= 4:
                            pth, r = divmod(dk - 4, 128)
                            poh, pi = divmod(r, 64)
                            W(g, PE, ("fdmm", tag, pth, poh, pi))
                        par = i % 4
                        ring_guard(g, DS[par], ("ds", par),
                                   ("fdg", tag, th, oh, i))
                        DS[par].inc(g.dma_start(
                            out=slabW[:, par * 2048:(par + 1) * 2048],
                            in_=wd_d.ap()[i * 128:(i + 1) * 128, :]), 16)
                        R(("fdg", tag, th, oh, i), DS[par].n)

        block.gpsimd(sec_gp)

        def sec_mm(t):
            for th in (0, 1):
                if th == 1:
                    W(t, DV, ("y", ytag, th))
                for j in range(NOG):
                    par = j % 2
                    lin = th * NOG + j
                    W(t, WS[par], ("fgG", tag, th, j))
                    for otl in (0, 1):
                        b = 2 * par + otl
                        if lin < 2:
                            use_bank(t, b)
                        else:
                            pth, pj = divmod(lin - 2, NOG)
                            W(t, AC, ("fsilu", tag, pth, pj, otl))
                        if th == 1 and j <= 1:
                            W(t, DV, ("ffnx", tag, 0, 1, b))
                        ins = None
                        for c in range(CD):
                            if th == 0 and j == 0 and otl == 0:
                                W(t, DV if c % 2 == 0 else GP,
                                  ("yac", ytag, c))
                            ins = t.matmul(
                                pb[b][:, :],
                                slabW[:, par * 4096 + c * 256 + otl * 128:
                                      par * 4096 + c * 256 + otl * 128 + 128],
                                y_bf[:, c * T + 512 * th:c * T + 512 * th + 512],
                                start=(c == 0), stop=(c == CD - 1))
                        PE.inc(ins)
                        R(("fgate", tag, th, j, otl), PE.n)
                    W(t, US[par], ("fg", tag, th, j))
                    for otl in (0, 1):
                        b = 4 + 2 * par + otl
                        if lin < 2:
                            use_bank(t, b)
                        else:
                            pth, pj = divmod(lin - 2, NOG)
                            W(t, DV, ("ffuse", tag, pth, pj, otl))
                        if th == 1 and j <= 1:
                            W(t, DV, ("ffnx", tag, 0, 1, b))
                        ins = None
                        for c in range(CD):
                            ins = t.matmul(
                                pb[b][:, :],
                                slabW[:, 8192 + par * 4096 + c * 256 + otl * 128:
                                      8192 + par * 4096 + c * 256 + otl * 128 + 128],
                                y_bf[:, c * T + 512 * th:c * T + 512 * th + 512],
                                start=(c == 0), stop=(c == CD - 1))
                        PE.inc(ins)
                        R(("fup", tag, th, j, otl), PE.n)
                W(t, AC, ("fsilu", tag, th, NOG - 1, 1))
                W(t, DV, ("ffuse", tag, th, NOG - 1, 1))
                if th == 0:
                    for b in range(8):
                        use_bank(t, b)
                for oh in (0, 1):
                    for i in range(NI):
                        par = i % 4
                        W(t, DS[par], ("fdg", tag, th, oh, i))
                        ins = None
                        for ot in range(8):
                            if i == 0 and (oh == 1 or th == 1):
                                pth = th if oh == 1 else th - 1
                                W(t, DV, ("ffnx", tag, pth,
                                          0 if oh == 1 else 1, ot))
                            ins = t.matmul(
                                pb[ot][:, :],
                                slabW[:, par * 2048 + oh * 1024 + ot * 128:
                                      par * 2048 + oh * 1024 + ot * 128 + 128],
                                slabA[:, i * 512:(i + 1) * 512],
                                start=(i == 0), stop=(i == NI - 1))
                            if i == NI - 1:
                                PE.inc(ins)
                                R(("fdmm2", tag, th, oh, ot), PE.n)
                        if i < NI - 1:
                            PE.inc(ins)
                        R(("fdmm", tag, th, oh, i), PE.n)
            R(("ffnw", tag), PE.n)
            R(("ffny", tag), rec[("fup", tag, 1, NOG - 1, 1)])

        block.tensor(sec_mm)

        def sec_act(s):
            W(s, SC, "scales")
            for th in (0, 1):
                for j in range(NOG):
                    par = j % 2
                    lin = th * NOG + j
                    for otl in (0, 1):
                        ot = j * 2 + otl
                        b = 2 * par + otl
                        W(s, PE, ("fgate", tag, th, j, otl))
                        if lin >= 1:
                            pth, pj = divmod(lin - 1, NOG)
                            W(s, DV, ("ffuse", tag, pth, pj, otl))
                        AC.inc(s.activation(gtmp[:, otl * 512:otl * 512 + 512],
                                            pb[b][:, :], AF.Silu,
                                            scale=sg_s[:, ot:ot + 1]))
                        R(("fsilu", tag, th, j, otl), AC.n)

        block.scalar(sec_act)

        def sec_vec(v):
            for th in (0, 1):
                for j in range(NOG):
                    par = j % 2
                    for otl in (0, 1):
                        ot = j * 2 + otl
                        b = 4 + 2 * par + otl
                        W(v, AC, ("fsilu", tag, th, j, otl))
                        W(v, PE, ("fup", tag, th, j, otl))
                        DV.inc(v.scalar_tensor_tensor(
                            slabA[:, ot * 512:(ot + 1) * 512],
                            pb[b][:, :], su_s[:, ot:ot + 1],
                            gtmp[:, otl * 512:otl * 512 + 512],
                            ALU.mult, ALU.mult))
                        R(("ffuse", tag, th, j, otl), DV.n)
                for oh in (0, 1):
                    for ot in range(8):
                        c = oh * 8 + ot
                        W(v, PE, ("fdmm2", tag, th, oh, ot))
                        DV.inc(v.scalar_tensor_tensor(
                            x_res[:, c * T + 512 * th:c * T + 512 * th + 512],
                            pb[ot][:, :], sdh_s[:, c:c + 1],
                            x_res[:, c * T + 512 * th:c * T + 512 * th + 512],
                            ALU.mult, ALU.add))
                        R(("ffnx", tag, th, oh, ot), DV.n)

        block.vector(sec_vec)
        for i in range(8):
            bank_free[i] = (DV, ("ffnx", tag, 1, 1, 7))

    def emit_qkv(ytag, w_free, gp_pre=None):
        def vgran_dma(eng, d, sem=None, cgs=range(4), record=True,
                      dbuf=None, doff=None):
            par = d % 2
            if sem is None:
                sem = QS[par]
            if dbuf is None:
                dbuf, doff = slabW, par * 8192
            for cg in cgs:
                src = wqkvT_d.ap()[cg * 512:(cg + 1) * 512,
                                   4096 + d * 512:4096 + d * 512 + 512].rearrange(
                    "(c p) o -> p c o", p=128)
                dst = dbuf[:, doff + cg * 2048:
                           doff + cg * 2048 + 2048
                           ].rearrange("p (c o) -> p c o", c=4)
                sem.inc(eng.dma_start(out=dst, in_=src), 16)
            if record:
                R(("vg", d), sem.n)
            return sem

        def sec_gp(g):
            jlist = list(range(8, 16)) + list(range(8))

            def qgran(idx):
                j = jlist[idx]
                if idx == 0:
                    W(g, PE, ("fup", "f1", 1, NOG - 2, 1))
                elif idx == 1:
                    W(g, PE, ("fup", "f1", 1, NOG - 1, 1))
                else:
                    W(g, PE, ("qkmm", jlist[idx - 2], 1, 1))
                par = idx % 2
                if idx >= 2:
                    W(g, QT[par], ("qgH", jlist[idx - 2]))
                    W(g, QS[par], ("qg", jlist[idx - 2]))
                for cg in range(4):
                    src = wqkvT_d.ap()[cg * 512:(cg + 1) * 512,
                                       j * 256:(j + 1) * 256].rearrange(
                        "(c p) o -> p c o", p=128)
                    dst = slabW[:, 8192 + par * 4096 + cg * 1024:
                                8192 + par * 4096 + cg * 1024 + 1024
                                ].rearrange("p (c o) -> p c o", c=4)
                    (QT[par] if cg < 2 else QS[par]).inc(
                        g.dma_start(out=dst, in_=src), 16)
                    if cg == 1:
                        R(("qgH", j), QT[par].n)
                R(("qg", j), QS[par].n)

            def vgather(d):
                W(g, VI[d], ("vin", 2 * d + 1))
                CC.inc(g.collective_compute(
                    "AllGather", ALU.bypass,
                    replica_groups=[list(range(NCORES))],
                    ins=[v_ins[d].ap()], outs=[v_outs[d].ap()]))

            def kgather(g2):
                W(g, KN if g2 == 0 else KM, ("kin", g2))
                CC.inc(g.collective_compute(
                    "AllGather", ALU.bypass,
                    replica_groups=[list(range(NCORES))],
                    ins=[k_ins[g2].ap()], outs=[k_outs[g2].ap()]))

            vstage = ln_scr[:, :].bitcast(BF16)

            def vbounce(u):
                d, ttg = divmod(u, 2)
                W(g, DV, ("ve", u, 3))
                if u % 2 == 1:
                    W(g, VI[d], ("vin", u - 1))
                dst = v_ins[d].ap()[ttg * 512:(ttg + 1) * 512, :].rearrange(
                    "(tt p) d2 -> p tt d2", p=128)
                vsrc = vstage[:, 0:2048].rearrange(
                    "p (tt d2) -> p tt d2", tt=4)
                VI[d].inc(g.dma_start(out=dst, in_=vsrc), 16)
                R(("vin", u), VI[d].n)

            qgran(0)
            qgran(1)
            if gp_pre is not None:
                gp_pre(g)
            qgran(2)
            qgran(3)
            W(g, PE, ("fdmm", "f1", 1, 1, NI - 1))
            vgran_dma(g, 0, sem=KB, cgs=(0, 1), record=False)
            qgran(4)
            qgran(5)
            vgran_dma(g, 0, sem=KB, cgs=(2, 3))
            for idx in range(6, 10):
                qgran(idx)
            kgather(0)
            W(g, PE, ("vmm", 1, 3))
            W(g, KB, ("vg", 0))
            vgran_dma(g, 2, sem=KB)
            for idx in range(10, 16):
                qgran(idx)
            vgather(0)
            kgather(1)
            W(g, KM, ("kin", 1))
            vgran_dma(g, 1, sem=VG, dbuf=slabA, doff=24576)
            W(g, PE, ("qkmm", jlist[15], 1, 1))
            W(g, KB, ("vg", 2))
            vgran_dma(g, 3, sem=KB)
            vbounce(2)
            vbounce(3)
            vgather(1)
            vbounce(4)
            vbounce(5)
            vgather(2)
            vbounce(6)
            vbounce(7)
            vgather(3)

        block.gpsimd(sec_gp)

        def sec_mm(t):
            jlist = list(range(8, 16)) + list(range(8))
            y_waited = set()

            def qk_idx(idx, ths=(0, 1)):
                j = jlist[idx]
                par = idx % 2
                W(t, QT[par], ("qgH", j))
                for otl in (0, 1):
                    for th in ths:
                        b = 4 * par + otl * 2 + th
                        first_scan = False
                        if th not in y_waited:
                            if th == 0:
                                first_scan = True
                            else:
                                W(t, DV, ("y", ytag, th))
                            y_waited.add(th)
                        if idx < 2:
                            use_bank(t, b)
                        elif idx in (8, 9):
                            W(t, DV, ("ve", idx - 8, 2 * otl + th))
                        else:
                            W(t, AC, ("qe", jlist[idx - 2], otl, th))
                        ins = None
                        for c in range(CD):
                            if c == 8:
                                W(t, QS[par], ("qg", j))
                            if first_scan:
                                W(t, DV if c % 2 == 0 else GP,
                                  ("yac", ytag, c))
                            ins = t.matmul(
                                pb[b][:, :],
                                slabW[:, 8192 + par * 4096 + c * 256 + otl * 128:
                                      8192 + par * 4096 + c * 256 + otl * 128 + 128],
                                y_bf[:, c * T + 512 * th:c * T + 512 * th + 512],
                                start=(c == 0), stop=(c == CD - 1))
                        PE.inc(ins)
                        R(("qkmm", j, otl, th), PE.n)

            def v_u(u):
                d, ttg = divmod(u, 2)
                wpar = d % 2
                if ttg == 0:
                    W(t, VG if d == 1 else KB, ("vg", d))
                par = u % 2
                for c in range(CD):
                    for tti in range(4):
                        tt = ttg * 4 + tti
                        b = 4 * par + tti
                        if c == 0:
                            if u < 2:
                                W(t, AC, ("qe", jlist[6 + u], tti >> 1, tti & 1))
                            elif u < 4:
                                W(t, AC, ("qe", jlist[12 + u], tti >> 1, tti & 1))
                            else:
                                W(t, DV, ("ve", u - 2, tti))
                        if d == 1:
                            wsrc = slabA[:, 24576 + c * 512:24576 + c * 512 + 512]
                        else:
                            wsrc = slabW[:, wpar * 8192 + c * 512:
                                         wpar * 8192 + c * 512 + 512]
                        ins = t.matmul(
                            pb[b][:, :],
                            y_bf[:, c * T + tt * 128:c * T + tt * 128 + 128],
                            wsrc,
                            start=(c == 0), stop=(c == CD - 1))
                        if c == CD - 1:
                            PE.inc(ins)
                            R(("vmm", u, tti), PE.n)

            qk_idx(0, (0,))
            qk_idx(1, (0,))
            qk_idx(0, (1,))
            qk_idx(1, (1,))
            for idx in range(2, 8):
                qk_idx(idx)
            v_u(0)
            v_u(1)
            for idx in range(8, 16):
                qk_idx(idx)
            for u in range(2, 8):
                v_u(u)
            R("qkv_pe_end", PE.n)

        block.tensor(sec_mm)

        def sec_qe(s):
            jlist = list(range(8, 16)) + list(range(8))
            order = ([(i, otl, 0) for i in (0, 1) for otl in (0, 1)]
                     + [(i, otl, 1) for i in (0, 1) for otl in (0, 1)]
                     + [(i, otl, th) for i in range(2, 16)
                        for otl in (0, 1) for th in (0, 1)])
            for idx, otl, th in order:
                j = jlist[idx]
                par = idx % 2
                ot = j * 2 + otl
                b = 4 * par + otl * 2 + th
                W(s, PE, ("qkmm", j, otl, th))
                AC.inc(s.activation(
                    slabA[:, ot * T + 512 * th:ot * T + 512 * th + 512],
                    pb[b][:, :], AF.Copy))
                R(("qe", j, otl, th), AC.n)

        block.scalar(sec_qe)

        def sec_vec(v):
            vstage = ln_scr[:, :].bitcast(BF16)
            for d in range(4):
                for ttg in (0, 1):
                    u = d * 2 + ttg
                    par = u % 2
                    for tti in range(4):
                        b = 4 * par + tti
                        W(v, PE, ("vmm", u, tti))
                        if u >= 1:
                            W(v, VI[(u - 1) // 2], ("vin", u - 1))
                        DV.inc(v.tensor_copy(
                            vstage[:, tti * 512:tti * 512 + 512],
                            pb[b][:, :]))
                        R(("ve", u, tti), DV.n)

        block.vector(sec_vec)

        def sec_sy(sy):
            for g2 in range(2):
                W(sy, AC, ("qe", 11 if g2 == 0 else 15, 1, 1))
                src = slabA[:, 16384 + g2 * 8192:16384 + (g2 + 1) * 8192
                            ].rearrange("p (ko t) -> p ko t", ko=8)
                dst = k_ins[g2].ap()[:, :].rearrange(
                    "(ko p) t -> p ko t", p=128)
                (KN if g2 == 0 else KM).inc(sy.dma_start(out=dst, in_=src), 16)
                R(("kin", g2), (KN if g2 == 0 else KM).n)
            vstage = ln_scr[:, :].bitcast(BF16)
            for u in range(2):
                d, ttg = divmod(u, 2)
                W(sy, DV, ("ve", u, 3))
                if u == 1:
                    W(sy, VI[d], ("vin", 0))
                dst = v_ins[d].ap()[ttg * 512:(ttg + 1) * 512, :].rearrange(
                    "(tt p) d2 -> p tt d2", p=128)
                vsrc = vstage[:, 0:2048].rearrange("p (tt d2) -> p tt d2", tt=4)
                VI[d].inc(sy.dma_start(out=dst, in_=vsrc), 16)
                R(("vin", u), VI[d].n)

        block.sync(sec_sy)
        for par in (0, 1):
            for tti in range(4):
                bank_free[4 * par + tti] = (DV, ("ve", 6 + par, tti))

    def emit_attn():
        NKT = S // 128

        def sec_sy(sy):
            cc_k = 0
            for h in range(NH):
                for half in (0, 1):
                    n = 2 * h + half
                    slot = n % 2
                    need = 1 if h < 8 else 3
                    if need > cc_k:
                        if real:
                            sy.wait_ge(CC.h, need)
                        cc_k = need
                    if n >= 2:
                        ph, phalf = divmod(n - 2, 2)
                        W(sy, PE, ("asc", ph, phalf * 32 + 31, 1))
                        W(sy, KR[slot], ("aKH", ph, phalf))
                    ksrc = k_outs[h // 8].ap().rearrange(
                        "(r d) t -> r d t", d=1024)[
                        half * 4:half * 4 + 4,
                        (h % 8) * 128:(h % 8) * 128 + 128, :
                    ].transpose([1, 0, 2])
                    kdst = slabA[:, KOFF + slot * 4096:KOFF + (slot + 1) * 4096
                                 ].rearrange("p (r t) -> p r t", r=4)
                    KR[slot].inc(sy.dma_start(out=kdst, in_=ksrc), 16)
                    R(("aKH", h, half), KR[slot].n)
                    if h % 4 == 0 and half == 0:
                        if real:
                            sy.wait_ge(CC.h, (2, 4, 5, 6)[h // 4])
                    if n >= 3:
                        ph, phalf = divmod(n - 3, 2)
                        W(sy, PE, ("apv", ph, phalf * 32 + 31, 1))
                        W(sy, VR[n % 3], ("aVH", ph, phalf))
                    elif n < 2:
                        W(sy, PE, ("vmm", 5, 3))
                    else:
                        W(sy, PE, ("vmm", 7, 3))
                    vslot3 = n % 3
                    vot = v_outs[h // 4]
                    for rl in range(4):
                        r = half * 4 + rl
                        vsrc = vot.ap()[r * T:(r + 1) * T,
                                        (h % 4) * 128:(h % 4) * 128 + 128
                                        ].rearrange("(k p) d -> p k d", p=128)
                        vdst = slabW[:, vslot3 * 4096 + rl * 1024:
                                     vslot3 * 4096 + rl * 1024 + 1024
                                     ].rearrange("p (k d) -> p k d", k=8)
                        VR[vslot3].inc(sy.dma_start(out=vdst, in_=vsrc), 16)
                    R(("aVH", h, half), VR[vslot3].n)

        block.sync(sec_sy)

        events = []
        for h in range(NH):
            for kt in range(NKT):
                events.append(("sc", h, kt))
                if h > 0 and kt == 3:
                    events.append(("den", h - 1))
                if h > 0 and kt == 16:
                    events.append(("rb", h - 1))
        events.append(("den", NH - 1))
        events.append(("rb", NH - 1))
        slot_of = []
        rel_ev = []
        prev_user = [None, None, None]
        for i, ev in enumerate(events):
            s = i % 3
            slot_of.append(s)
            rel_ev.append(prev_user[s])
            prev_user[s] = i
        sc_slot = {}
        den_slot = {}
        rb_slot = {}
        for i, ev in enumerate(events):
            if ev[0] == "sc":
                sc_slot[(ev[1], ev[2])] = slot_of[i]
            elif ev[0] == "den":
                den_slot[ev[1]] = slot_of[i]
            else:
                rb_slot[ev[1]] = slot_of[i]

        bcf = slabA[:, :].bitcast(F32)

        def pacc_a(qh):
            return slabA[:, PAOFF + qh * 512:PAOFF + qh * 512 + 512]

        def pacc_b(qh):
            return bcf[:, 15360 + qh * 512:15360 + qh * 512 + 512]

        def pacc_bh(qh):
            return slabA[:, 29696 + qh * 512:29696 + qh * 512 + 512]

        def sec_mm(t):
            W(t, PE, "qkv_pe_end")

            def rel_wait(i):
                j = rel_ev[i]
                if j is None:
                    s = slot_of[i]
                    use_bank(t, 2 * s)
                    use_bank(t, 2 * s + 1)
                    return
                pe_ = events[j]
                if pe_[0] == "sc":
                    W(t, AC, ("axp", pe_[1], pe_[2]))
                elif pe_[0] == "den":
                    W(t, AC, ("aln", pe_[1]))
                else:
                    W(t, DV, ("aao", pe_[1]))

            def pv_mm(h, kt):
                half = kt // 32
                vslot = (2 * h + half) % 3
                W(t, AC, ("axp", h, kt))
                if kt % 32 == 0:
                    W(t, VR[vslot], ("aVH", h, half))
                vcol = (vslot * 4096 + ((kt // 8) % 4) * 1024
                        + (kt % 8) * 128)
                for qh in (0, 1):
                    if kt == 0:
                        if h == 0:
                            use_bank(t, 6 + qh)
                        else:
                            W(t, DV, ("apc", h - 1))
                    PE.inc(t.matmul(
                        pb[6 + qh][:, :],
                        slabW[:, vcol:vcol + 128],
                        slabA[:, POFF + (kt % 4) * 1024 + qh * 512:
                              POFF + (kt % 4) * 1024 + qh * 512 + 512],
                        start=(kt == 0), stop=(kt == NKT - 1)))
                    R(("apv", h, kt, qh), PE.n)

            sc_lin = []
            for i, ev in enumerate(events):
                if ev[0] == "sc":
                    _, h, kt = ev
                    s = slot_of[i]
                    if kt == 0:
                        W(t, AC, ("qe", h // 2, h % 2, 1))
                    half = kt // 32
                    kslot = (2 * h + half) % 2
                    if kt % 32 == 0:
                        W(t, KR[kslot], ("aKH", h, half))
                    kcol = (KOFF + kslot * 4096 + ((kt // 8) % 4) * 1024
                            + (kt % 8) * 128)
                    rel_wait(i)
                    for qh in (0, 1):
                        PE.inc(t.matmul(
                            pp[s][:, qh * 512:qh * 512 + 512],
                            slabA[:, kcol:kcol + 128],
                            slabA[:, h * T + qh * 512:h * T + qh * 512 + 512],
                            start=True, stop=True))
                        R(("asc", h, kt, qh), PE.n)
                    sc_lin.append((h, kt))
                    if len(sc_lin) >= 5:
                        pv_mm(*sc_lin[-5])
                elif ev[0] == "den":
                    hh = ev[1]
                    if hh == NH - 1:
                        for tr in sc_lin[-4:]:
                            pv_mm(*tr)
                    s = slot_of[i]
                    rel_wait(i)
                    W(t, DV, ("apbh", hh))
                    for qh in (0, 1):
                        PE.inc(t.matmul(
                            pp[s][0:1, qh * 512:qh * 512 + 512],
                            ones_b[:, 0:1], pacc_bh(qh),
                            start=True, stop=True))
                    R(("aden2", hh), PE.n)
                else:
                    hh = ev[1]
                    s = slot_of[i]
                    rel_wait(i)
                    W(t, AC, ("arecip", hh))
                    sbf = stats[0:1, :].bitcast(BF16)
                    for qh in (0, 1):
                        PE.inc(t.matmul(
                            pp[s][:, qh * 512:qh * 512 + 512],
                            onesk1b[0:1, :],
                            sbf[0:1, 2048 + qh * 512:2560 + qh * 512],
                            start=True, stop=True))
                    R(("arb", hh), PE.n)
            R("attn_pe_end", PE.n)

        block.tensor(sec_mm)

        def sec_act(s):
            def act_tail(hh):
                W(s, PE, ("aden2", hh))
                ds = den_slot[hh]
                sbf = stats[0:1, :].bitcast(BF16)
                for qh in (0, 1):
                    AC.inc(s.activation(stats[0:1, qh * 512:qh * 512 + 512],
                                        pp[ds][0:1, qh * 512:qh * 512 + 512],
                                        AF.Ln, bias=cst[0:1, 1:2]))
                R(("aln", hh), AC.n)
                for qh in (0, 1):
                    if real:
                        s.wait_ge(AC.h, AC.n)
                    AC.inc(s.activation(
                        sbf[0:1, 2048 + qh * 512:2560 + qh * 512],
                        stats[0:1, qh * 512:qh * 512 + 512],
                        AF.Exp, scale=-1.0))
                R(("arecip", hh), AC.n)

            for h in range(NH):
                for kt in range(NKT):
                    W(s, PE, ("asc", h, kt, 1))
                    pk = h * NKT + kt
                    if pk >= 4:
                        pph, ppkt = divmod(pk - 4, NKT)
                        W(s, PE, ("apv", pph, ppkt, 1))
                        W(s, DV, ("apa", pph, ppkt))
                    AC.inc(s.activation(
                        slabA[:, POFF + (kt % 4) * 1024:
                              POFF + (kt % 4) * 1024 + 1024],
                        pp[sc_slot[(h, kt)]][:, :], AF.Exp,
                        scale=float(SCALE)))
                    R(("axp", h, kt), AC.n)
                    if kt == 4 and h > 0:
                        act_tail(h - 1)
            act_tail(NH - 1)

        block.scalar(sec_act)

        def sec_vec(v):
            def aao(hh):
                W(v, PE, ("arb", hh))
                rs = rb_slot[hh]
                for qh in (0, 1):
                    if real:
                        v.wait_ge(DV.h, DV.n)
                    DV.inc(v.tensor_tensor(
                        y_bf[:, hh * T + qh * 512:hh * T + qh * 512 + 512],
                        y_bf[:, hh * T + qh * 512:hh * T + qh * 512 + 512],
                        pp[rs][:, qh * 512:qh * 512 + 512], ALU.mult))
                R(("aao", hh), DV.n)

            def apc(hh):
                if hh == 0:
                    W(v, PE, "qkv_pe_end")
                for qh in (0, 1):
                    W(v, PE, ("apv", hh, NKT - 1, qh))
                    if real:
                        v.wait_ge(DV.h, DV.n)
                    DV.inc(v.tensor_copy(
                        y_bf[:, hh * T + qh * 512:hh * T + qh * 512 + 512],
                        pb[6 + qh][:, :]))
                R(("apc", hh), DV.n)

            paccA = slabA[:, PAOFF:PAOFF + 1024]
            paccB = bcf[:, 15360:16384]
            paccBH = slabA[:, 29696:30720]
            for h in range(NH):
                for kt in range(NKT):
                    g = kt // 8
                    W(v, AC, ("axp", h, kt))
                    if real:
                        v.wait_ge(DV.h, DV.n)
                    psrc = slabA[:, POFF + (kt % 4) * 1024:
                                 POFF + (kt % 4) * 1024 + 1024]
                    if kt % 8 == 0:
                        DV.inc(v.tensor_copy(paccA, psrc))
                    else:
                        DV.inc(v.tensor_add(paccA, paccA, psrc))
                    R(("apa", h, kt), DV.n)
                    if kt % 8 == 7:
                        if real:
                            v.wait_ge(DV.h, DV.n)
                        if g == 0:
                            DV.inc(v.tensor_copy(paccB, paccA))
                        else:
                            DV.inc(v.tensor_add(paccB, paccB, paccA))
                        R(("apb", h, g), DV.n)
                        if g == 7:
                            if real:
                                v.wait_ge(DV.h, DV.n)
                            DV.inc(v.tensor_copy(paccBH, paccB))
                            R(("apbh", h), DV.n)
                    if h > 0 and kt == 1:
                        apc(h - 1)
                    if h > 0 and kt == 13:
                        aao(h - 1)
            apc(NH - 1)
            aao(NH - 1)

        block.vector(sec_vec)
        for s in range(3):
            i = prev_user[s]
            ev = events[i]
            if ev[0] == "sc":
                rel = (AC, ("axp", ev[1], ev[2]))
            elif ev[0] == "den":
                rel = (AC, ("aln", ev[1]))
            else:
                rel = (DV, ("aao", ev[1]))
            bank_free[2 * s] = rel
            bank_free[2 * s + 1] = rel
        bank_free[6] = (DV, ("apc", NH - 1))
        bank_free[7] = (DV, ("apc", NH - 1))

    def emit_oproj():
        jlist2 = list(range(8, 16)) + list(range(8))

        def sec_gp(g):
            for j in range(8):
                if j == 0:
                    W(g, PE, ("vmm", 7, 3))
                elif j == 1:
                    W(g, PE, ("apv", 14, 63, 1))
                else:
                    W(g, PE, ("omm", j - 2, 1, 1))
                par = j % 2
                if j < 2:
                    W(g, QS[par], ("qg", jlist2[14 + j]))
                else:
                    W(g, QS[par], ("og", j - 2))
                ob = 12288 if par == 0 else 8192
                for cg in range(4):
                    src = woutT_d.ap()[cg * 512:(cg + 1) * 512,
                                       j * 256:(j + 1) * 256].rearrange(
                        "(c p) o -> p c o", p=128)
                    dst = slabW[:, ob + cg * 1024:ob + cg * 1024 + 1024
                                ].rearrange("p (c o) -> p c o", c=4)
                    QS[par].inc(g.dma_start(out=dst, in_=src), 16)
                R(("og", j), QS[par].n)
                if j == 1:
                    for jj in (0, 1):
                        W(g, PE, ("apv", NH - 1, 31 if jj == 0 else 63, 1))
                        gpar = jj % 2
                        ring_guard(g, WS[gpar], ("ws", gpar),
                                   ("fgG", "f2", 0, jj))
                        for cg in range(4):
                            src = wg_d.ap()[cg * 512:(cg + 1) * 512,
                                            jj * 256:(jj + 1) * 256].rearrange(
                                "(c p) o -> p c o", p=128)
                            dst = slabW[:, gpar * 4096 + cg * 1024:
                                        gpar * 4096 + cg * 1024 + 1024
                                        ].rearrange("p (c o) -> p c o", c=4)
                            WS[gpar].inc(g.dma_start(out=dst, in_=src), 16)
                        R(("fgG", "f2", 0, jj), WS[gpar].n)

        block.gpsimd(sec_gp)

        def sec_mm(t):
            W(t, DV, ("aao", NH - 1))
            for j in range(8):
                par = j % 2
                W(t, QS[par], ("og", j))
                for otl in (0, 1):
                    for th in (0, 1):
                        b = 4 * par + otl * 2 + th
                        if j < 2:
                            use_bank(t, b)
                        else:
                            W(t, DV, ("oe", j - 2, otl, th))
                    ob = 12288 if par == 0 else 8192
                    for c in range(CD):
                        for th in (0, 1):
                            b = 4 * par + otl * 2 + th
                            ins = t.matmul(
                                pb[b][:, :],
                                slabW[:, ob + c * 256 + otl * 128:
                                      ob + c * 256 + otl * 128 + 128],
                                y_bf[:, c * T + 512 * th:c * T + 512 * th + 512],
                                start=(c == 0), stop=(c == CD - 1))
                            if c == CD - 1:
                                PE.inc(ins)
                                R(("omm", j, otl, th), PE.n)
            R("oproj_pe_end", PE.n)

        block.tensor(sec_mm)

        def sec_vec(v):
            for j in range(8):
                par = j % 2
                for otl in (0, 1):
                    c = j * 2 + otl
                    for th in (0, 1):
                        b = 4 * par + otl * 2 + th
                        W(v, PE, ("omm", j, otl, th))
                        DV.inc(v.tensor_tensor(
                            x_res[:, c * T + 512 * th:c * T + 512 * th + 512],
                            pb[b][:, :],
                            x_res[:, c * T + 512 * th:c * T + 512 * th + 512],
                            ALU.add))
                        R(("oe", j, otl, th), DV.n)

        block.vector(sec_vec)
        for par in (0, 1):
            for otl in (0, 1):
                for th in (0, 1):
                    bank_free[4 * par + otl * 2 + th] = (DV, ("oe", 6 + par, otl, th))

    def emit_store(x_ready):
        def store_c(eng, c, sem):
            sv = x_ready(c)
            if sv is not None:
                W(eng, sv[0], sv[1])
            sem.inc(eng.dma_start(
                out=out_d.ap()[c * 128:(c + 1) * 128, :],
                in_=x_res[:, c * T:(c + 1) * T]), 16)

        def sec_sy(sy):
            for c in range(0, CD, 2):
                store_c(sy, c, SX)
            R("store_sx", SX.n)
            if real:
                sy.wait_ge(SX.h, rec["store_sx"])
                sy.wait_ge(GX.h, rec["store_gx"])

        def sec_gp(g):
            for c in range(1, CD, 2):
                store_c(g, c, GX)
            R("store_gx", GX.n)
            if real:
                g.wait_ge(SX.h, rec["store_sx"])
                g.wait_ge(GX.h, rec["store_gx"])

        block.sync(sec_sy)
        block.gpsimd(sec_gp)

    gp1 = emit_ln("ln1", lambda c: (SX if c < 8 else GX, ("xg", c // 8)), None)
    emit_ffn("f1", "ln1", None, gp_pre=gp1)
    if _BUILD_UPTO >= 2:
        gp2 = emit_ln("ln2", lambda c: (DV, ("ffnx", "f1", 1, c // 8, c % 8)),
                      (PE, ("ffny", "f1")))
        emit_qkv("ln2", None, gp_pre=gp2)
    if _BUILD_UPTO >= 3:
        emit_attn()
        emit_oproj()
    if _BUILD_UPTO >= 4:
        gp3 = emit_ln("ln3", lambda c: (DV, ("oe", c // 2, c % 2, 1)),
                      (PE, "oproj_pe_end"))
        emit_ffn("f2", "ln3", None, skip_first_gates=True, gp_pre=gp3)

    if _BUILD_UPTO >= 4:
        emit_store(lambda c: (DV, ("ffnx", "f2", 1, c // 8, c % 8)))
    elif _BUILD_UPTO == 3:
        emit_store(lambda c: (DV, ("oe", c // 2, c % 2, 1)))
    else:
        emit_store(lambda c: (DV, ("ffnx", "f1", 1, c // 8, c % 8)))

    if _BUILD_UPTO == 2:
        def sec_tail(g):
            if real:
                g.wait_ge(CC.h, 5)
        block.gpsimd(sec_tail)

    return {i: s.n for i, s in enumerate(sems)}


def build_program():
    nc = bass.Bass("TRN2", target_bir_lowering=False)

    x_d = nc.dram_tensor("x", [DM, T], F32, kind="ExternalInput")
    wqkvT_d = nc.dram_tensor("wqkvT", [DM, 3 * DM], F32, kind="ExternalInput")
    woutT_d = nc.dram_tensor("woutT", [DM, DM], F32, kind="ExternalInput")
    wg_d = nc.dram_tensor("wg", [DM, IN], I8, kind="ExternalInput")
    wu_d = nc.dram_tensor("wu", [DM, IN], I8, kind="ExternalInput")
    wd_d = nc.dram_tensor("wd", [IN, DM], I8, kind="ExternalInput")
    sg_d = nc.dram_tensor("sg", [128, NI], F32, kind="ExternalInput")
    su_d = nc.dram_tensor("su", [128, NI], F32, kind="ExternalInput")
    sd_d = nc.dram_tensor("sd", [128, CD], F32, kind="ExternalInput")
    out_d = nc.dram_tensor("out", [DM, T], F32, kind="ExternalOutput")

    k_ins = [nc.dram_tensor(f"k_in{i}", [1024, T], BF16) for i in range(2)]
    k_outs = [nc.dram_tensor(f"k_out{i}", [NCORES * 1024, T], BF16,
                             addr_space="Shared") for i in range(2)]
    v_ins = [nc.dram_tensor(f"v_in{i}", [T, 512], BF16) for i in range(4)]
    v_outs = [nc.dram_tensor(f"v_out{i}", [NCORES * T, 512], BF16,
                             addr_space="Shared") for i in range(4)]

    with ExitStack() as stk:
        e = stk.enter_context
        x_res = e(nc.sbuf_tensor("x_res", [128, CD * T], F32))
        y_bf = e(nc.sbuf_tensor("y_bf", [128, CD * T], BF16))
        slabA = e(nc.sbuf_tensor("slabA", [128, 32768], BF16))
        slabW = e(nc.sbuf_tensor("slabW", [128, 16384], BF16))
        ln_scr = e(nc.sbuf_tensor("ln_scr", [128, 1024], F32))
        stats = e(nc.sbuf_tensor("stats", [1, 2048], F32))
        gtmp = e(nc.sbuf_tensor("gtmp", [128, 1024], BF16))
        sg_s = e(nc.sbuf_tensor("sg_s", [128, NI], F32))
        su_s = e(nc.sbuf_tensor("su_s", [128, NI], F32))
        sd_s = e(nc.sbuf_tensor("sd_s", [128, CD], F32))
        sdh_s = e(nc.sbuf_tensor("sdh_s", [128, CD], F32))
        ones_b = e(nc.sbuf_tensor("ones_b", [128, 1], BF16))
        ones_f = e(nc.sbuf_tensor("ones_f", [128, 1], F32))
        onesk1 = e(nc.sbuf_tensor("onesk1", [1, 128], F32))
        onesk1b = e(nc.sbuf_tensor("onesk1b", [1, 128], BF16))
        cst = e(nc.sbuf_tensor("cst", [1, 2], F32))
        pp = [e(nc.psum_tensor(f"pp{i}", [128, 1024], F32)) for i in range(4)]
        pb = [pp[i // 2][:, (i % 2) * 512:(i % 2) * 512 + 512] for i in range(8)]

        tensors = (x_res, y_bf, slabA, slabW, ln_scr, stats, gtmp, sg_s, su_s,
                   sd_s, sdh_s, ones_b, ones_f, onesk1, onesk1b, cst, pb, pp,
                   x_d, wqkvT_d, woutT_d, wg_d, wu_d, wd_d, sg_d, su_d, sd_d,
                   out_d, k_ins, k_outs, v_ins, v_outs)

        rec = {}
        SEM_NAMES = ["SX", "SC", "KB", "VA", "VB", "VC", "VD",
                     "WA", "WB", "QA", "QB", "DA", "DB", "DC", "DD",
                     "KR0", "KR1", "KR2", "VR0", "VR1", "VR2",
                     "CC", "PE", "AC", "DV", "GP", "KN", "GX", "QT0", "QT1",
                     "UA", "UB", "KM", "VG"]
        plan_sems = tuple(PlanSem() for _ in SEM_NAMES)
        plan_counts = emit_all(_FakeBlock(), nc, tensors, plan_sems, rec,
                               real=False)

        real_sems = tuple(Sem(stk, nc, n) for n in SEM_NAMES)
        block = e(nc.Block())
        real_counts = emit_all(block, nc, tensors, real_sems, rec, real=True)
        assert plan_counts == real_counts, (plan_counts, real_counts)

    return nc


_NC_CACHE = None


def _enable_ldw_opt():
    try:
        import libneuronxla.libncc as ncc
        for i, f in enumerate(ncc.NEURON_CC_FLAGS):
            if f.startswith('--internal-backend-options='):
                ncc.NEURON_CC_FLAGS[i] = f.replace(
                    '--enable-ldw-opt=false', '--enable-ldw-opt=true')
    except Exception:
        pass


def _get_program():
    global _NC_CACHE
    if _NC_CACHE is None:
        _enable_ldw_opt()
        _NC_CACHE = build_program()
    return _NC_CACHE


def kernel(L0_attn_x, L0_attn_w_qkv, L0_attn_w_out, L0_ffn_w_gate,
           L0_ffn_w_up, L0_ffn_w_down, L0_ffn_s_gate, L0_ffn_s_up,
           L0_ffn_s_down):
    x = np.asarray(L0_attn_x)
    wqkvT = np.ascontiguousarray(np.asarray(L0_attn_w_qkv).T)
    woutT = np.ascontiguousarray(np.asarray(L0_attn_w_out).T)
    wg = np.ascontiguousarray(np.asarray(L0_ffn_w_gate))
    wu = np.ascontiguousarray(np.asarray(L0_ffn_w_up))
    wd = np.ascontiguousarray(np.asarray(L0_ffn_w_down))
    sg = np.ascontiguousarray(np.asarray(L0_ffn_s_gate).reshape(NI, 128).T)
    su = np.ascontiguousarray(np.asarray(L0_ffn_s_up).reshape(NI, 128).T)
    sd = np.ascontiguousarray(np.asarray(L0_ffn_s_down).reshape(CD, 128).T)

    nc = _get_program()
    in_maps = []
    for c in range(NCORES):
        xc = np.ascontiguousarray(x[0, c * T:(c + 1) * T, :].T)
        in_maps.append({"x": xc, "wqkvT": wqkvT, "woutT": woutT,
                        "wg": wg, "wu": wu, "wd": wd,
                        "sg": sg, "su": su, "sd": sd})
    res = run_bass_kernel_spmd(nc, in_maps, core_ids=list(range(NCORES)))
    out = np.concatenate([np.asarray(r["out"]).T for r in res.results], axis=0)
    return np.ascontiguousarray(out).reshape(1, S, DM)


# version 43
# speedup vs baseline: 1.0143x; 1.0134x over previous
import numpy as np
import concourse.bass as bass
import concourse.mybir as mybir
from concourse.bass_utils import run_bass_kernel_spmd
from contextlib import ExitStack

F32 = mybir.dt.float32
BF16 = mybir.dt.bfloat16
I8 = mybir.dt.int8
AF = mybir.ActivationFunctionType
ALU = mybir.AluOpType

NCORES = 8
T = 1024
DM = 2048
CD = DM // 128
IN = 8192
NI = IN // 128
NOG = IN // 256
NH = 16
S = NCORES * T
SCALE = 1.0 / float(np.sqrt(128.0))
LN_EPS = 1e-5

_BUILD_UPTO = 4

KOFF = 16384
POFF = 24576
PAOFF = 28672


class PlanSem:
    __slots__ = ("h", "n")

    def __init__(self):
        self.h = None
        self.n = 0

    def inc(self, inst, k=1):
        self.n += k
        return self.n


class Sem:
    def __init__(self, stack, nc, name):
        self.h = stack.enter_context(nc.semaphore(name))
        self.n = 0

    def inc(self, inst, k=1):
        inst.then_inc(self.h, k)
        self.n += k
        return self.n


class _FakeInst:
    def then_inc(self, *a, **k):
        return self


class _FakeEng:
    def __getattr__(self, name):
        def f(*a, **k):
            return _FakeInst()
        return f


class _FakeBlock:
    def __getattr__(self, name):
        def runner(f):
            f(_FakeEng())
        return runner


def emit_all(block, nc, tensors, sems, rec, real):
    (x_res, y_bf, slabA, slabW, ln_scr, stats, gtmp, sg_s, su_s, sd_s,
     sdh_s, ones_b, ones_f, onesk1, onesk1b, cst, pb, pp,
     x_d, wqkvT_d, woutT_d, wg_d, wu_d, wd_d, sg_d, su_d, sd_d, out_d,
     k_ins, k_outs, v_ins, v_outs) = tensors
    (SX, SC, KB, VA, VB, VC, VD, WA, WB, QA, QB, DA, DB, DC, DD,
     KR0, KR1, KR2, VR0, VR1, VR2, CC, PE, AC, DV, GP, KN, GX,
     QT0, QT1, UA, UB, KM, VG) = sems
    QT = [QT0, QT1]
    US = [UA, UB]
    VI = [VA, VB, VC, VD]
    KR = [KR0, KR1, KR2]
    VR = [VR0, VR1, VR2]
    WS = [WA, WB]
    QS = [QA, QB]
    DS = [DA, DB, DC, DD]

    def W(eng, sem, key):
        if real:
            eng.wait_ge(sem.h, rec[key])

    def R(key, val):
        if real:
            assert rec[key] == val, (key, rec[key], val)
        rec[key] = val

    bank_free = {}
    ring_last = {}

    def ring_guard(eng, sem, name, key):
        prev = ring_last.get(name)
        if prev is not None:
            W(eng, sem, prev)
        ring_last[name] = key

    def use_bank(eng, i):
        if i in bank_free and bank_free[i] is not None:
            sem, key = bank_free[i]
            W(eng, sem, key)
            bank_free[i] = None

    def xload(eng, g, sem):
        src = x_d.ap()[g * 512:(g + 1) * 512, :].rearrange(
            "(c p) t -> p c t", p=128)
        dst = x_res[:, g * 4 * T:(g + 1) * 4 * T].rearrange(
            "p (c t) -> p c t", c=4)
        sem.inc(eng.dma_start(out=dst, in_=src), 16)
        if g % 2 == 1:
            R(("xg", g // 2), sem.n)

    def sec_init_sync(sy):
        xload(sy, 0, SX)
        xload(sy, 1, SX)
        SC.inc(sy.dma_start(out=sg_s[:], in_=sg_d.ap()), 16)
        SC.inc(sy.dma_start(out=su_s[:], in_=su_d.ap()), 16)
        SC.inc(sy.dma_start(out=sd_s[:], in_=sd_d.ap()), 16)
        R("scales", SC.n)

    block.sync(sec_init_sync)

    def sec_init_gp(g):
        xload(g, 2, GX)
        xload(g, 3, GX)

    block.gpsimd(sec_init_gp)

    def sec_init_vec(v):
        v.memset(ones_b[:], 1.0)
        v.memset(ones_f[:], 1.0)
        v.memset(onesk1[:], 1.0)
        v.memset(onesk1b[:], 1.0)
        v.memset(cst[0:1, 0:1], LN_EPS)
        v.memset(cst[0:1, 1:2], 1e-8)
        W(v, SC, "scales")
        DV.inc(v.tensor_scalar(sdh_s[:], sd_s[:], 0.5, None, ALU.mult))
        R("init_dv", DV.n)

    block.vector(sec_init_vec)

    def emit_ln(tag, x_ready, y_free):
        bcf = slabA[:, :].bitcast(F32)

        def bcA(th):
            return bcf[:, 14336 + th * 512:14336 + th * 512 + 512]

        def bcB(th):
            return bcf[:, 15360 + th * 512:15360 + th * 512 + 512]
        def sec_sq(s):
            for c in range(CD):
                if x_ready is not None:
                    sv = x_ready(c)
                    if sv is not None:
                        W(s, sv[0], sv[1])
                for th in (0, 1):
                    if c >= 1:
                        W(s, PE, ("lnst2", tag, c - 1, th))
                    AC.inc(s.activation(
                        gtmp[:, 512 * th:512 * th + 512],
                        x_res[:, c * T + 512 * th:c * T + 512 * th + 512],
                        AF.Square))
                    R(("lnsq2", tag, c, th), AC.n)

        def sec_cp(v):
            xbf = ln_scr[:, :].bitcast(BF16)
            for c in range(CD):
                if x_ready is not None:
                    sv = x_ready(c)
                    if sv is not None:
                        W(v, sv[0], sv[1])
                for th in (0, 1):
                    sl = (c % 2) * 2 + th
                    if c >= 2:
                        W(v, PE, ("lnst2", tag, c - 2, th))
                    DV.inc(v.tensor_copy(
                        xbf[:, sl * 512:sl * 512 + 512],
                        x_res[:, c * T + 512 * th:c * T + 512 * th + 512]))
                    R(("lncp", tag, c, th), DV.n)

        def sec_rstd(s):
            for th in (0, 1):
                W(s, DV, ("lnvar", tag, th))
                if real:
                    s.wait_ge(AC.h, AC.n)
                AC.inc(s.activation(ln_scr[0:1, 512:1024],
                                    stats[0:1, 1024 + 512 * th:1536 + 512 * th],
                                    AF.Ln, bias=cst[0:1, 0:1]))
                if real:
                    s.wait_ge(AC.h, AC.n)
                AC.inc(s.activation(
                    stats[0:1, :].bitcast(BF16)[0:1, 2048 + 512 * th:
                                                2560 + 512 * th],
                    ln_scr[0:1, 512:1024], AF.Exp, scale=-0.5))
                R(("lnrstd", tag, th), AC.n)
            for th in (0, 1):
                W(s, PE, ("lnbc", tag, th))
                AC.inc(s.activation(bcA(th), pb[4 + th][:, :], AF.Copy))
                AC.inc(s.activation(bcB(th), pb[6 + th][:, :], AF.Copy))
                R(("lncpy", tag, th), AC.n)

        block.vector(sec_cp)
        block.scalar(sec_sq)
        block.scalar(sec_rstd)

        def sec_mm(t):
            W(t, DV, "init_dv")
            for i in range(4):
                use_bank(t, i)
            xbf = ln_scr[:, :].bitcast(BF16)
            for c in range(CD):
                for th in (0, 1):
                    sl = (c % 2) * 2 + th
                    W(t, AC, ("lnsq2", tag, c, th))
                    W(t, DV, ("lncp", tag, c, th))
                    t.matmul(pb[th][0:1, :], ones_b[:, 0:1],
                             xbf[:, sl * 512:sl * 512 + 512],
                             start=(c == 0), stop=(c == CD - 1))
                    ins = t.matmul(pb[2 + th][0:1, :], ones_b[:, 0:1],
                                   gtmp[:, 512 * th:512 * th + 512],
                                   start=(c == 0), stop=(c == CD - 1))
                    PE.inc(ins)
                    R(("lnst2", tag, c, th), PE.n)
                R(("lnst", tag, c), PE.n)
            sbf = stats[0:1, :].bitcast(BF16)
            for th in (0, 1):
                use_bank(t, 4 + th)
                use_bank(t, 6 + th)
                W(t, DV, ("lnm", tag, th))
                t.matmul(pb[4 + th][:, :], onesk1b[0:1, :],
                         slabA[0:1, 27648 + 512 * th:27648 + 512 * th + 512],
                         start=True, stop=True)
                W(t, AC, ("lnrstd", tag, th))
                PE.inc(t.matmul(pb[6 + th][:, :], onesk1b[0:1, :],
                                sbf[0:1, 2048 + 512 * th:2560 + 512 * th],
                                start=True, stop=True))
                R(("lnbc", tag, th), PE.n)

        block.tensor(sec_mm)

        def sec_vec(v):
            mbf = slabA[0:1, :]
            W(v, PE, ("lnst", tag, CD - 1))
            for th in (0, 1):
                DV.inc(v.tensor_scalar(
                    mbf[0:1, 27648 + 512 * th:27648 + 512 * th + 512],
                    pb[th][0:1, :], 1.0 / DM, None, ALU.mult))
                R(("lnm", tag, th), DV.n)
            for th in (0, 1):
                DV.inc(v.tensor_scalar(stats[0:1, 1024 + 512 * th:1536 + 512 * th],
                                       pb[2 + th][0:1, :], 1.0 / DM, None,
                                       ALU.mult))
            R(("lnmsq", tag), DV.n)
            for th in (0, 1):
                if real:
                    v.wait_ge(DV.h, DV.n)
                DV.inc(v.tensor_tensor(
                    stats[0:1, 0:512],
                    mbf[0:1, 27648 + 512 * th:27648 + 512 * th + 512],
                    mbf[0:1, 27648 + 512 * th:27648 + 512 * th + 512],
                    ALU.mult))
                if real:
                    v.wait_ge(DV.h, DV.n)
                DV.inc(v.tensor_tensor(stats[0:1, 1024 + 512 * th:1536 + 512 * th],
                                       stats[0:1, 1024 + 512 * th:1536 + 512 * th],
                                       stats[0:1, 0:512], ALU.subtract))
                R(("lnvar", tag, th), DV.n)
            first = True
            for th in (0, 1):
                W(v, AC, ("lncpy", tag, th))
                for c in range(CD):
                    if th == 0 and c % 2 == 1:
                        continue
                    if first and y_free is not None:
                        W(v, y_free[0], y_free[1])
                    first = False
                    scr = ln_scr[:, (c % 2) * 512:(c % 2) * 512 + 512]
                    if real:
                        v.wait_ge(DV.h, DV.n)
                    DV.inc(v.tensor_tensor(
                        scr,
                        x_res[:, c * T + 512 * th:c * T + 512 * th + 512],
                        bcA(th), ALU.subtract))
                    if real:
                        v.wait_ge(DV.h, DV.n)
                    DV.inc(v.tensor_tensor(
                        y_bf[:, c * T + 512 * th:c * T + 512 * th + 512],
                        scr, bcB(th), ALU.mult))
                    if th == 0:
                        R(("yac", tag, c), DV.n)
                R(("y", tag, th), DV.n)

        block.vector(sec_vec)

        def gp_apply(g):
            if y_free is not None:
                W(g, y_free[0], y_free[1])
            W(g, AC, ("lncpy", tag, 0))
            gscr = gtmp[:, :].bitcast(F32)
            for c in range(1, CD, 2):
                if real:
                    g.wait_ge(GP.h, GP.n)
                GP.inc(g.tensor_tensor(
                    gscr[:, 0:512],
                    x_res[:, c * T:c * T + 512],
                    bcA(0), ALU.subtract))
                if real:
                    g.wait_ge(GP.h, GP.n)
                GP.inc(g.tensor_tensor(
                    y_bf[:, c * T:c * T + 512],
                    gscr[:, 0:512], bcB(0), ALU.mult))
                R(("yac", tag, c), GP.n)
            R(("yg0", tag), GP.n)
        for i in range(4):
            bank_free[i] = (DV, ("lnmsq", tag))
        for i in range(4, 8):
            bank_free[i] = (AC, ("lncpy", tag, 1))
        return gp_apply

    def emit_ffn(tag, ytag, w_free, skip_first_gates=False, gp_pre=None):
        def sec_gp(g):
            if w_free is not None:
                W(g, w_free[0], w_free[1])
            if skip_first_gates:
                for jj, key in ((1, ("omm", 6, 1, 1)), (0, ("omm", 7, 1, 1))):
                    W(g, PE, key)
                    par = jj % 2
                    ring_guard(g, US[par], ("us", par), ("fg", tag, 0, jj))
                    for cg in range(4):
                        src = wu_d.ap()[cg * 512:(cg + 1) * 512,
                                        jj * 256:(jj + 1) * 256].rearrange(
                            "(c p) o -> p c o", p=128)
                        dst = slabW[:, 8192 + par * 4096 + cg * 1024:
                                    8192 + par * 4096 + cg * 1024 + 1024
                                    ].rearrange("p (c o) -> p c o", c=4)
                        US[par].inc(g.dma_start(out=dst, in_=src), 16)
                    R(("fg", tag, 0, jj), US[par].n)
                if gp_pre is not None:
                    gp_pre(g)
            for th in (0, 1):
                for j in range(NOG):
                    lin = th * NOG + j
                    if skip_first_gates and lin < 2:
                        continue
                    par = j % 2
                    if th == 1 and j <= 1:
                        W(g, PE, ("fdmm", tag, 0, 1, 61 if j == 0 else 63))
                        ring_guard(g, WS[par], ("ws", par),
                                   ("fgG", tag, th, j))
                        for cg in range(4):
                            src = wg_d.ap()[cg * 512:(cg + 1) * 512,
                                            j * 256:(j + 1) * 256].rearrange(
                                "(c p) o -> p c o", p=128)
                            dst = slabW[:, par * 4096 + cg * 1024:
                                        par * 4096 + cg * 1024 + 1024
                                        ].rearrange("p (c o) -> p c o", c=4)
                            WS[par].inc(g.dma_start(out=dst, in_=src), 16)
                        R(("fgG", tag, th, j), WS[par].n)
                        continue
                    elif lin >= 2:
                        pth, pj = divmod(lin - 2, NOG)
                        W(g, PE, ("fup", tag, pth, pj, 1))
                    for w_dd, off in ((wg_d, 0), (wu_d, 8192)):
                        wsem = WS[par] if w_dd is wg_d else US[par]
                        if w_dd is wg_d:
                            ring_guard(g, wsem, ("ws", par),
                                       ("fgG", tag, th, j))
                        else:
                            ring_guard(g, wsem, ("us", par),
                                       ("fg", tag, th, j))
                        for cg in range(4):
                            src = w_dd.ap()[cg * 512:(cg + 1) * 512,
                                            j * 256:(j + 1) * 256].rearrange(
                                "(c p) o -> p c o", p=128)
                            dst = slabW[:, off + par * 4096 + cg * 1024:
                                        off + par * 4096 + cg * 1024 + 1024
                                        ].rearrange("p (c o) -> p c o", c=4)
                            wsem.inc(g.dma_start(out=dst, in_=src), 16)
                        if w_dd is wg_d:
                            R(("fgG", tag, th, j), wsem.n)
                    R(("fg", tag, th, j), US[par].n)
                    if gp_pre is not None and th == 0 and j == 1:
                        gp_pre(g)
                def up_half(jj):
                    W(g, PE, ("fup", tag, 0, NOG - 2 + jj, 1))
                    par = jj % 2
                    ring_guard(g, US[par], ("us", par), ("fg", tag, 1, jj))
                    for cg in range(4):
                        src = wu_d.ap()[cg * 512:(cg + 1) * 512,
                                        jj * 256:(jj + 1) * 256].rearrange(
                            "(c p) o -> p c o", p=128)
                        dst = slabW[:, 8192 + par * 4096 + cg * 1024:
                                    8192 + par * 4096 + cg * 1024 + 1024
                                    ].rearrange("p (c o) -> p c o", c=4)
                        US[par].inc(g.dma_start(out=dst, in_=src), 16)
                    R(("fg", tag, 1, jj), US[par].n)
                for oh in (0, 1):
                    for i in range(NI):
                        if th == 0 and oh == 0 and i in (2, 4):
                            up_half(0 if i == 2 else 1)
                        dk = th * 128 + oh * 64 + i
                        if oh == 0 and i < 4:
                            W(g, PE, ("fgate", tag, th,
                                      NOG - 2 if i < 2 else NOG - 1, 1))
                        if dk >= 4:
                            pth, r = divmod(dk - 4, 128)
                            poh, pi = divmod(r, 64)
                            W(g, PE, ("fdmm", tag, pth, poh, pi))
                        par = i % 4
                        ring_guard(g, DS[par], ("ds", par),
                                   ("fdg", tag, th, oh, i))
                        DS[par].inc(g.dma_start(
                            out=slabW[:, par * 2048:(par + 1) * 2048],
                            in_=wd_d.ap()[i * 128:(i + 1) * 128, :]), 16)
                        R(("fdg", tag, th, oh, i), DS[par].n)

        block.gpsimd(sec_gp)

        def sec_mm(t):
            for th in (0, 1):
                if th == 1:
                    W(t, DV, ("y", ytag, th))
                for j in range(NOG):
                    par = j % 2
                    lin = th * NOG + j
                    W(t, WS[par], ("fgG", tag, th, j))
                    for otl in (0, 1):
                        b = 2 * par + otl
                        if lin < 2:
                            use_bank(t, b)
                        else:
                            pth, pj = divmod(lin - 2, NOG)
                            W(t, AC, ("fsilu", tag, pth, pj, otl))
                        if th == 1 and j <= 1:
                            W(t, DV, ("ffnx", tag, 0, 1, b))
                        ins = None
                        for c in range(CD):
                            if th == 0 and j == 0 and otl == 0:
                                W(t, DV if c % 2 == 0 else GP,
                                  ("yac", ytag, c))
                            ins = t.matmul(
                                pb[b][:, :],
                                slabW[:, par * 4096 + c * 256 + otl * 128:
                                      par * 4096 + c * 256 + otl * 128 + 128],
                                y_bf[:, c * T + 512 * th:c * T + 512 * th + 512],
                                start=(c == 0), stop=(c == CD - 1))
                        PE.inc(ins)
                        R(("fgate", tag, th, j, otl), PE.n)
                    W(t, US[par], ("fg", tag, th, j))
                    for otl in (0, 1):
                        b = 4 + 2 * par + otl
                        if lin < 2:
                            use_bank(t, b)
                        else:
                            pth, pj = divmod(lin - 2, NOG)
                            W(t, DV, ("ffuse", tag, pth, pj, otl))
                        if th == 1 and j <= 1:
                            W(t, DV, ("ffnx", tag, 0, 1, b))
                        ins = None
                        for c in range(CD):
                            ins = t.matmul(
                                pb[b][:, :],
                                slabW[:, 8192 + par * 4096 + c * 256 + otl * 128:
                                      8192 + par * 4096 + c * 256 + otl * 128 + 128],
                                y_bf[:, c * T + 512 * th:c * T + 512 * th + 512],
                                start=(c == 0), stop=(c == CD - 1))
                        PE.inc(ins)
                        R(("fup", tag, th, j, otl), PE.n)
                W(t, AC, ("fsilu", tag, th, NOG - 1, 1))
                W(t, DV, ("ffuse", tag, th, NOG - 1, 1))
                if th == 0:
                    for b in range(8):
                        use_bank(t, b)
                for oh in (0, 1):
                    for i in range(NI):
                        par = i % 4
                        W(t, DS[par], ("fdg", tag, th, oh, i))
                        ins = None
                        for ot in range(8):
                            if i == 0 and (oh == 1 or th == 1):
                                pth = th if oh == 1 else th - 1
                                W(t, DV, ("ffnx", tag, pth,
                                          0 if oh == 1 else 1, ot))
                            ins = t.matmul(
                                pb[ot][:, :],
                                slabW[:, par * 2048 + oh * 1024 + ot * 128:
                                      par * 2048 + oh * 1024 + ot * 128 + 128],
                                slabA[:, i * 512:(i + 1) * 512],
                                start=(i == 0), stop=(i == NI - 1))
                            if i == NI - 1:
                                PE.inc(ins)
                                R(("fdmm2", tag, th, oh, ot), PE.n)
                        if i < NI - 1:
                            PE.inc(ins)
                        R(("fdmm", tag, th, oh, i), PE.n)
            R(("ffnw", tag), PE.n)
            R(("ffny", tag), rec[("fup", tag, 1, NOG - 1, 1)])

        block.tensor(sec_mm)

        def sec_act(s):
            W(s, SC, "scales")
            for th in (0, 1):
                for j in range(NOG):
                    par = j % 2
                    lin = th * NOG + j
                    for otl in (0, 1):
                        ot = j * 2 + otl
                        b = 2 * par + otl
                        W(s, PE, ("fgate", tag, th, j, otl))
                        if lin >= 1:
                            pth, pj = divmod(lin - 1, NOG)
                            W(s, DV, ("ffuse", tag, pth, pj, otl))
                        AC.inc(s.activation(gtmp[:, otl * 512:otl * 512 + 512],
                                            pb[b][:, :], AF.Silu,
                                            scale=sg_s[:, ot:ot + 1]))
                        R(("fsilu", tag, th, j, otl), AC.n)

        block.scalar(sec_act)

        def sec_vec(v):
            for th in (0, 1):
                for j in range(NOG):
                    par = j % 2
                    for otl in (0, 1):
                        ot = j * 2 + otl
                        b = 4 + 2 * par + otl
                        W(v, AC, ("fsilu", tag, th, j, otl))
                        W(v, PE, ("fup", tag, th, j, otl))
                        DV.inc(v.scalar_tensor_tensor(
                            slabA[:, ot * 512:(ot + 1) * 512],
                            pb[b][:, :], su_s[:, ot:ot + 1],
                            gtmp[:, otl * 512:otl * 512 + 512],
                            ALU.mult, ALU.mult))
                        R(("ffuse", tag, th, j, otl), DV.n)
                for oh in (0, 1):
                    for ot in range(8):
                        c = oh * 8 + ot
                        W(v, PE, ("fdmm2", tag, th, oh, ot))
                        DV.inc(v.scalar_tensor_tensor(
                            x_res[:, c * T + 512 * th:c * T + 512 * th + 512],
                            pb[ot][:, :], sdh_s[:, c:c + 1],
                            x_res[:, c * T + 512 * th:c * T + 512 * th + 512],
                            ALU.mult, ALU.add))
                        R(("ffnx", tag, th, oh, ot), DV.n)

        block.vector(sec_vec)
        for i in range(8):
            bank_free[i] = (DV, ("ffnx", tag, 1, 1, 7))

    def emit_qkv(ytag, w_free, gp_pre=None):
        def vgran_dma(eng, d, sem=None, cgs=range(4), record=True,
                      dbuf=None, doff=None):
            par = d % 2
            if sem is None:
                sem = QS[par]
            if dbuf is None:
                dbuf, doff = slabW, par * 8192
            for cg in cgs:
                src = wqkvT_d.ap()[cg * 512:(cg + 1) * 512,
                                   4096 + d * 512:4096 + d * 512 + 512].rearrange(
                    "(c p) o -> p c o", p=128)
                dst = dbuf[:, doff + cg * 2048:
                           doff + cg * 2048 + 2048
                           ].rearrange("p (c o) -> p c o", c=4)
                sem.inc(eng.dma_start(out=dst, in_=src), 16)
            if record:
                R(("vg", d), sem.n)
            return sem

        def sec_gp(g):
            jlist = list(range(8, 16)) + list(range(8))

            def qgran(idx):
                j = jlist[idx]
                if idx == 0:
                    W(g, PE, ("fup", "f1", 1, NOG - 2, 1))
                elif idx == 1:
                    W(g, PE, ("fup", "f1", 1, NOG - 1, 1))
                else:
                    W(g, PE, ("qkmm", jlist[idx - 2], 1, 1))
                par = idx % 2
                if idx >= 2:
                    W(g, QT[par], ("qgH", jlist[idx - 2]))
                    W(g, QS[par], ("qg", jlist[idx - 2]))
                for cg in range(4):
                    src = wqkvT_d.ap()[cg * 512:(cg + 1) * 512,
                                       j * 256:(j + 1) * 256].rearrange(
                        "(c p) o -> p c o", p=128)
                    dst = slabW[:, 8192 + par * 4096 + cg * 1024:
                                8192 + par * 4096 + cg * 1024 + 1024
                                ].rearrange("p (c o) -> p c o", c=4)
                    (QT[par] if cg < 2 else QS[par]).inc(
                        g.dma_start(out=dst, in_=src), 16)
                    if cg == 1:
                        R(("qgH", j), QT[par].n)
                R(("qg", j), QS[par].n)

            def vgather(d):
                W(g, VI[d], ("vin", 2 * d + 1))
                CC.inc(g.collective_compute(
                    "AllGather", ALU.bypass,
                    replica_groups=[list(range(NCORES))],
                    ins=[v_ins[d].ap()], outs=[v_outs[d].ap()]))

            def kgather(g2):
                W(g, KN if g2 == 0 else KM, ("kin", g2))
                CC.inc(g.collective_compute(
                    "AllGather", ALU.bypass,
                    replica_groups=[list(range(NCORES))],
                    ins=[k_ins[g2].ap()], outs=[k_outs[g2].ap()]))

            vstage = ln_scr[:, :].bitcast(BF16)

            def vbounce(u):
                d, ttg = divmod(u, 2)
                W(g, DV, ("ve", u, 3))
                if u % 2 == 1:
                    W(g, VI[d], ("vin", u - 1))
                dst = v_ins[d].ap()[ttg * 512:(ttg + 1) * 512, :].rearrange(
                    "(tt p) d2 -> p tt d2", p=128)
                vsrc = vstage[:, 0:2048].rearrange(
                    "p (tt d2) -> p tt d2", tt=4)
                VI[d].inc(g.dma_start(out=dst, in_=vsrc), 16)
                R(("vin", u), VI[d].n)

            qgran(0)
            qgran(1)
            if gp_pre is not None:
                gp_pre(g)
            qgran(2)
            qgran(3)
            W(g, PE, ("fdmm", "f1", 1, 1, NI - 1))
            vgran_dma(g, 0, sem=KB, cgs=(0, 1), record=False)
            qgran(4)
            qgran(5)
            vgran_dma(g, 0, sem=KB, cgs=(2, 3))
            for idx in range(6, 10):
                qgran(idx)
            kgather(0)
            W(g, PE, ("vmm", 1, 3))
            W(g, KB, ("vg", 0))
            vgran_dma(g, 2, sem=KB)
            for idx in range(10, 16):
                qgran(idx)
            vgather(0)
            kgather(1)
            W(g, KM, ("kin", 1))
            vgran_dma(g, 1, sem=VG, dbuf=slabA, doff=24576)
            W(g, PE, ("qkmm", jlist[15], 1, 1))
            W(g, KB, ("vg", 2))
            vgran_dma(g, 3, sem=KB)
            vbounce(2)
            vbounce(3)
            vgather(1)
            vbounce(4)
            vbounce(5)
            vgather(2)
            vbounce(6)
            vbounce(7)
            vgather(3)

        block.gpsimd(sec_gp)

        def sec_mm(t):
            jlist = list(range(8, 16)) + list(range(8))
            y_waited = set()

            def qk_idx(idx, ths=(0, 1)):
                j = jlist[idx]
                par = idx % 2
                W(t, QT[par], ("qgH", j))
                for otl in (0, 1):
                    for th in ths:
                        b = 4 * par + otl * 2 + th
                        first_scan = False
                        if th not in y_waited:
                            if th == 0:
                                first_scan = True
                            else:
                                W(t, DV, ("y", ytag, th))
                            y_waited.add(th)
                        if idx < 2:
                            use_bank(t, b)
                        elif idx in (8, 9):
                            W(t, DV, ("ve", idx - 8, 2 * otl + th))
                        else:
                            W(t, AC, ("qe", jlist[idx - 2], otl, th))
                        ins = None
                        for c in range(CD):
                            if c == 8:
                                W(t, QS[par], ("qg", j))
                            if first_scan:
                                W(t, DV if c % 2 == 0 else GP,
                                  ("yac", ytag, c))
                            ins = t.matmul(
                                pb[b][:, :],
                                slabW[:, 8192 + par * 4096 + c * 256 + otl * 128:
                                      8192 + par * 4096 + c * 256 + otl * 128 + 128],
                                y_bf[:, c * T + 512 * th:c * T + 512 * th + 512],
                                start=(c == 0), stop=(c == CD - 1))
                        PE.inc(ins)
                        R(("qkmm", j, otl, th), PE.n)

            def v_u(u):
                d, ttg = divmod(u, 2)
                wpar = d % 2
                if ttg == 0:
                    W(t, VG if d == 1 else KB, ("vg", d))
                par = u % 2
                for c in range(CD):
                    for tti in range(4):
                        tt = ttg * 4 + tti
                        b = 4 * par + tti
                        if c == 0:
                            if u < 2:
                                W(t, AC, ("qe", jlist[6 + u], tti >> 1, tti & 1))
                            elif u < 4:
                                W(t, AC, ("qe", jlist[12 + u], tti >> 1, tti & 1))
                            else:
                                W(t, DV, ("ve", u - 2, tti))
                        if d == 1:
                            wsrc = slabA[:, 24576 + c * 512:24576 + c * 512 + 512]
                        else:
                            wsrc = slabW[:, wpar * 8192 + c * 512:
                                         wpar * 8192 + c * 512 + 512]
                        ins = t.matmul(
                            pb[b][:, :],
                            y_bf[:, c * T + tt * 128:c * T + tt * 128 + 128],
                            wsrc,
                            start=(c == 0), stop=(c == CD - 1))
                        if c == CD - 1:
                            PE.inc(ins)
                            R(("vmm", u, tti), PE.n)

            qk_idx(0, (0,))
            qk_idx(1, (0,))
            qk_idx(0, (1,))
            qk_idx(1, (1,))
            for idx in range(2, 8):
                qk_idx(idx)
            v_u(0)
            v_u(1)
            for idx in range(8, 16):
                qk_idx(idx)
            for u in range(2, 8):
                v_u(u)
            R("qkv_pe_end", PE.n)

        block.tensor(sec_mm)

        def sec_qe(s):
            jlist = list(range(8, 16)) + list(range(8))
            order = ([(i, otl, 0) for i in (0, 1) for otl in (0, 1)]
                     + [(i, otl, 1) for i in (0, 1) for otl in (0, 1)]
                     + [(i, otl, th) for i in range(2, 16)
                        for otl in (0, 1) for th in (0, 1)])
            for idx, otl, th in order:
                j = jlist[idx]
                par = idx % 2
                ot = j * 2 + otl
                b = 4 * par + otl * 2 + th
                W(s, PE, ("qkmm", j, otl, th))
                AC.inc(s.activation(
                    slabA[:, ot * T + 512 * th:ot * T + 512 * th + 512],
                    pb[b][:, :], AF.Copy))
                R(("qe", j, otl, th), AC.n)

        block.scalar(sec_qe)

        def sec_vec(v):
            vstage = ln_scr[:, :].bitcast(BF16)
            for d in range(4):
                for ttg in (0, 1):
                    u = d * 2 + ttg
                    par = u % 2
                    for tti in range(4):
                        b = 4 * par + tti
                        W(v, PE, ("vmm", u, tti))
                        if u >= 1:
                            W(v, VI[(u - 1) // 2], ("vin", u - 1))
                        DV.inc(v.tensor_copy(
                            vstage[:, tti * 512:tti * 512 + 512],
                            pb[b][:, :]))
                        R(("ve", u, tti), DV.n)

        block.vector(sec_vec)

        def sec_sy(sy):
            for g2 in range(2):
                W(sy, AC, ("qe", 11 if g2 == 0 else 15, 1, 1))
                src = slabA[:, 16384 + g2 * 8192:16384 + (g2 + 1) * 8192
                            ].rearrange("p (ko t) -> p ko t", ko=8)
                dst = k_ins[g2].ap()[:, :].rearrange(
                    "(ko p) t -> p ko t", p=128)
                (KN if g2 == 0 else KM).inc(sy.dma_start(out=dst, in_=src), 16)
                R(("kin", g2), (KN if g2 == 0 else KM).n)
            vstage = ln_scr[:, :].bitcast(BF16)
            for u in range(2):
                d, ttg = divmod(u, 2)
                W(sy, DV, ("ve", u, 3))
                if u == 1:
                    W(sy, VI[d], ("vin", 0))
                dst = v_ins[d].ap()[ttg * 512:(ttg + 1) * 512, :].rearrange(
                    "(tt p) d2 -> p tt d2", p=128)
                vsrc = vstage[:, 0:2048].rearrange("p (tt d2) -> p tt d2", tt=4)
                VI[d].inc(sy.dma_start(out=dst, in_=vsrc), 16)
                R(("vin", u), VI[d].n)

        block.sync(sec_sy)
        for par in (0, 1):
            for tti in range(4):
                bank_free[4 * par + tti] = (DV, ("ve", 6 + par, tti))

    def emit_attn():
        NKT = S // 128

        def sec_sy(sy):
            cc_k = 0
            for h in range(NH):
                for half in (0, 1):
                    n = 2 * h + half
                    slot = n % 2
                    need = 1 if h < 8 else 3
                    if need > cc_k:
                        if real:
                            sy.wait_ge(CC.h, need)
                        cc_k = need
                    if n >= 2:
                        ph, phalf = divmod(n - 2, 2)
                        W(sy, PE, ("asc", ph, phalf * 32 + 31, 1))
                        W(sy, KR[slot], ("aKH", ph, phalf))
                    ksrc = k_outs[h // 8].ap().rearrange(
                        "(r d) t -> r d t", d=1024)[
                        half * 4:half * 4 + 4,
                        (h % 8) * 128:(h % 8) * 128 + 128, :
                    ].transpose([1, 0, 2])
                    kdst = slabA[:, KOFF + slot * 4096:KOFF + (slot + 1) * 4096
                                 ].rearrange("p (r t) -> p r t", r=4)
                    KR[slot].inc(sy.dma_start(out=kdst, in_=ksrc), 16)
                    R(("aKH", h, half), KR[slot].n)
                    if h % 4 == 0 and half == 0:
                        if real:
                            sy.wait_ge(CC.h, (2, 4, 5, 6)[h // 4])
                    if n >= 3:
                        ph, phalf = divmod(n - 3, 2)
                        W(sy, PE, ("apv", ph, phalf * 32 + 31, 1))
                        W(sy, VR[n % 3], ("aVH", ph, phalf))
                    elif n < 2:
                        W(sy, PE, ("vmm", 5, 3))
                    else:
                        W(sy, PE, ("vmm", 7, 3))
                    vslot3 = n % 3
                    vot = v_outs[h // 4]
                    for rl in range(4):
                        r = half * 4 + rl
                        vsrc = vot.ap()[r * T:(r + 1) * T,
                                        (h % 4) * 128:(h % 4) * 128 + 128
                                        ].rearrange("(k p) d -> p k d", p=128)
                        vdst = slabW[:, vslot3 * 4096 + rl * 1024:
                                     vslot3 * 4096 + rl * 1024 + 1024
                                     ].rearrange("p (k d) -> p k d", k=8)
                        VR[vslot3].inc(sy.dma_start(out=vdst, in_=vsrc), 16)
                    R(("aVH", h, half), VR[vslot3].n)

        block.sync(sec_sy)

        events = []
        for h in range(NH):
            for kt in range(NKT):
                events.append(("sc", h, kt))
                if h > 0 and kt == 3:
                    events.append(("den", h - 1))
                if h > 0 and kt == 16:
                    events.append(("rb", h - 1))
        events.append(("den", NH - 1))
        events.append(("rb", NH - 1))
        slot_of = []
        rel_ev = []
        prev_user = [None, None, None]
        for i, ev in enumerate(events):
            s = i % 3
            slot_of.append(s)
            rel_ev.append(prev_user[s])
            prev_user[s] = i
        sc_slot = {}
        den_slot = {}
        rb_slot = {}
        for i, ev in enumerate(events):
            if ev[0] == "sc":
                sc_slot[(ev[1], ev[2])] = slot_of[i]
            elif ev[0] == "den":
                den_slot[ev[1]] = slot_of[i]
            else:
                rb_slot[ev[1]] = slot_of[i]

        bcf = slabA[:, :].bitcast(F32)

        def pacc_a(qh):
            return slabA[:, PAOFF + qh * 512:PAOFF + qh * 512 + 512]

        def pacc_b(qh):
            return bcf[:, 15360 + qh * 512:15360 + qh * 512 + 512]

        def pacc_bh(qh):
            return slabA[:, 29696 + qh * 512:29696 + qh * 512 + 512]

        def sec_mm(t):
            W(t, PE, "qkv_pe_end")

            def rel_wait(i):
                j = rel_ev[i]
                if j is None:
                    s = slot_of[i]
                    use_bank(t, 2 * s)
                    use_bank(t, 2 * s + 1)
                    return
                pe_ = events[j]
                if pe_[0] == "sc":
                    W(t, AC, ("axp", pe_[1], pe_[2]))
                elif pe_[0] == "den":
                    W(t, AC, ("aln", pe_[1]))
                else:
                    W(t, DV, ("aao", pe_[1]))

            def pv_mm(h, kt):
                half = kt // 32
                vslot = (2 * h + half) % 3
                W(t, AC, ("axp", h, kt))
                if kt % 32 == 0:
                    W(t, VR[vslot], ("aVH", h, half))
                vcol = (vslot * 4096 + ((kt // 8) % 4) * 1024
                        + (kt % 8) * 128)
                for qh in (0, 1):
                    if kt == 0:
                        if h == 0:
                            use_bank(t, 6 + qh)
                        else:
                            W(t, DV, ("apc", h - 1))
                    PE.inc(t.matmul(
                        pb[6 + qh][:, :],
                        slabW[:, vcol:vcol + 128],
                        slabA[:, POFF + (kt % 4) * 1024 + qh * 512:
                              POFF + (kt % 4) * 1024 + qh * 512 + 512],
                        start=(kt == 0), stop=(kt == NKT - 1)))
                    R(("apv", h, kt, qh), PE.n)

            sc_lin = []
            for i, ev in enumerate(events):
                if ev[0] == "sc":
                    _, h, kt = ev
                    s = slot_of[i]
                    if kt == 0:
                        W(t, AC, ("qe", h // 2, h % 2, 1))
                    half = kt // 32
                    kslot = (2 * h + half) % 2
                    if kt % 32 == 0:
                        W(t, KR[kslot], ("aKH", h, half))
                    kcol = (KOFF + kslot * 4096 + ((kt // 8) % 4) * 1024
                            + (kt % 8) * 128)
                    rel_wait(i)
                    for qh in (0, 1):
                        PE.inc(t.matmul(
                            pp[s][:, qh * 512:qh * 512 + 512],
                            slabA[:, kcol:kcol + 128],
                            slabA[:, h * T + qh * 512:h * T + qh * 512 + 512],
                            start=True, stop=True))
                        R(("asc", h, kt, qh), PE.n)
                    sc_lin.append((h, kt))
                    if len(sc_lin) >= 5:
                        pv_mm(*sc_lin[-5])
                elif ev[0] == "den":
                    hh = ev[1]
                    if hh == NH - 1:
                        for tr in sc_lin[-4:]:
                            pv_mm(*tr)
                    s = slot_of[i]
                    rel_wait(i)
                    W(t, DV, ("apbh", hh))
                    for qh in (0, 1):
                        PE.inc(t.matmul(
                            pp[s][0:1, qh * 512:qh * 512 + 512],
                            ones_b[:, 0:1], pacc_bh(qh),
                            start=True, stop=True))
                    R(("aden2", hh), PE.n)
                else:
                    hh = ev[1]
                    s = slot_of[i]
                    rel_wait(i)
                    W(t, AC, ("arecip", hh))
                    sbf = stats[0:1, :].bitcast(BF16)
                    for qh in (0, 1):
                        PE.inc(t.matmul(
                            pp[s][:, qh * 512:qh * 512 + 512],
                            onesk1b[0:1, :],
                            sbf[0:1, 2048 + qh * 512:2560 + qh * 512],
                            start=True, stop=True))
                    R(("arb", hh), PE.n)
            R("attn_pe_end", PE.n)

        block.tensor(sec_mm)

        def sec_act(s):
            def act_tail(hh):
                W(s, PE, ("aden2", hh))
                ds = den_slot[hh]
                sbf = stats[0:1, :].bitcast(BF16)
                for qh in (0, 1):
                    AC.inc(s.activation(stats[0:1, qh * 512:qh * 512 + 512],
                                        pp[ds][0:1, qh * 512:qh * 512 + 512],
                                        AF.Ln, bias=cst[0:1, 1:2]))
                R(("aln", hh), AC.n)
                for qh in (0, 1):
                    if real:
                        s.wait_ge(AC.h, AC.n)
                    AC.inc(s.activation(
                        sbf[0:1, 2048 + qh * 512:2560 + qh * 512],
                        stats[0:1, qh * 512:qh * 512 + 512],
                        AF.Exp, scale=-1.0))
                R(("arecip", hh), AC.n)

            for h in range(NH):
                for kt in range(NKT):
                    W(s, PE, ("asc", h, kt, 1))
                    pk = h * NKT + kt
                    if pk >= 4:
                        pph, ppkt = divmod(pk - 4, NKT)
                        W(s, PE, ("apv", pph, ppkt, 1))
                        W(s, DV, ("apa", pph, ppkt))
                    AC.inc(s.activation(
                        slabA[:, POFF + (kt % 4) * 1024:
                              POFF + (kt % 4) * 1024 + 1024],
                        pp[sc_slot[(h, kt)]][:, :], AF.Exp,
                        scale=float(SCALE)))
                    R(("axp", h, kt), AC.n)
                    if kt == 3 and h > 0:
                        act_tail(h - 1)
            act_tail(NH - 1)

        block.scalar(sec_act)

        def sec_vec(v):
            def aao(hh):
                W(v, PE, ("arb", hh))
                rs = rb_slot[hh]
                for qh in (0, 1):
                    if real:
                        v.wait_ge(DV.h, DV.n)
                    DV.inc(v.tensor_tensor(
                        y_bf[:, hh * T + qh * 512:hh * T + qh * 512 + 512],
                        y_bf[:, hh * T + qh * 512:hh * T + qh * 512 + 512],
                        pp[rs][:, qh * 512:qh * 512 + 512], ALU.mult))
                R(("aao", hh), DV.n)

            def apc(hh):
                if hh == 0:
                    W(v, PE, "qkv_pe_end")
                for qh in (0, 1):
                    W(v, PE, ("apv", hh, NKT - 1, qh))
                    if real:
                        v.wait_ge(DV.h, DV.n)
                    DV.inc(v.tensor_copy(
                        y_bf[:, hh * T + qh * 512:hh * T + qh * 512 + 512],
                        pb[6 + qh][:, :]))
                R(("apc", hh), DV.n)

            paccA = slabA[:, PAOFF:PAOFF + 1024]
            paccB = bcf[:, 15360:16384]
            paccBH = slabA[:, 29696:30720]
            for h in range(NH):
                for kt in range(NKT):
                    g = kt // 8
                    W(v, AC, ("axp", h, kt))
                    if real:
                        v.wait_ge(DV.h, DV.n)
                    psrc = slabA[:, POFF + (kt % 4) * 1024:
                                 POFF + (kt % 4) * 1024 + 1024]
                    if kt % 8 == 0:
                        DV.inc(v.tensor_copy(paccA, psrc))
                    else:
                        DV.inc(v.tensor_add(paccA, paccA, psrc))
                    R(("apa", h, kt), DV.n)
                    if kt % 8 == 7:
                        if real:
                            v.wait_ge(DV.h, DV.n)
                        if g == 0:
                            DV.inc(v.tensor_copy(paccB, paccA))
                        else:
                            DV.inc(v.tensor_add(paccB, paccB, paccA))
                        R(("apb", h, g), DV.n)
                        if g == 7:
                            if real:
                                v.wait_ge(DV.h, DV.n)
                            DV.inc(v.tensor_copy(paccBH, paccB))
                            R(("apbh", h), DV.n)
                    if h > 0 and kt == 1:
                        apc(h - 1)
                    if h > 0 and kt == 13:
                        aao(h - 1)
            apc(NH - 1)
            aao(NH - 1)

        block.vector(sec_vec)
        for s in range(3):
            i = prev_user[s]
            ev = events[i]
            if ev[0] == "sc":
                rel = (AC, ("axp", ev[1], ev[2]))
            elif ev[0] == "den":
                rel = (AC, ("aln", ev[1]))
            else:
                rel = (DV, ("aao", ev[1]))
            bank_free[2 * s] = rel
            bank_free[2 * s + 1] = rel
        bank_free[6] = (DV, ("apc", NH - 1))
        bank_free[7] = (DV, ("apc", NH - 1))

    def emit_oproj():
        jlist2 = list(range(8, 16)) + list(range(8))

        def sec_gp(g):
            for j in range(8):
                if j == 0:
                    W(g, PE, ("vmm", 7, 3))
                elif j == 1:
                    W(g, PE, ("apv", 14, 63, 1))
                else:
                    W(g, PE, ("omm", j - 2, 1, 1))
                par = j % 2
                if j < 2:
                    W(g, QS[par], ("qg", jlist2[14 + j]))
                else:
                    W(g, QS[par], ("og", j - 2))
                ob = 12288 if par == 0 else 8192
                for cg in range(4):
                    src = woutT_d.ap()[cg * 512:(cg + 1) * 512,
                                       j * 256:(j + 1) * 256].rearrange(
                        "(c p) o -> p c o", p=128)
                    dst = slabW[:, ob + cg * 1024:ob + cg * 1024 + 1024
                                ].rearrange("p (c o) -> p c o", c=4)
                    QS[par].inc(g.dma_start(out=dst, in_=src), 16)
                R(("og", j), QS[par].n)
                if j == 1:
                    for jj in (0, 1):
                        W(g, PE, ("apv", NH - 1, 31 if jj == 0 else 63, 1))
                        gpar = jj % 2
                        ring_guard(g, WS[gpar], ("ws", gpar),
                                   ("fgG", "f2", 0, jj))
                        for cg in range(4):
                            src = wg_d.ap()[cg * 512:(cg + 1) * 512,
                                            jj * 256:(jj + 1) * 256].rearrange(
                                "(c p) o -> p c o", p=128)
                            dst = slabW[:, gpar * 4096 + cg * 1024:
                                        gpar * 4096 + cg * 1024 + 1024
                                        ].rearrange("p (c o) -> p c o", c=4)
                            WS[gpar].inc(g.dma_start(out=dst, in_=src), 16)
                        R(("fgG", "f2", 0, jj), WS[gpar].n)

        block.gpsimd(sec_gp)

        def sec_mm(t):
            W(t, DV, ("aao", NH - 1))
            for j in range(8):
                par = j % 2
                W(t, QS[par], ("og", j))
                for otl in (0, 1):
                    for th in (0, 1):
                        b = 4 * par + otl * 2 + th
                        if j < 2:
                            use_bank(t, b)
                        else:
                            W(t, DV, ("oe", j - 2, otl, th))
                    ob = 12288 if par == 0 else 8192
                    for c in range(CD):
                        for th in (0, 1):
                            b = 4 * par + otl * 2 + th
                            ins = t.matmul(
                                pb[b][:, :],
                                slabW[:, ob + c * 256 + otl * 128:
                                      ob + c * 256 + otl * 128 + 128],
                                y_bf[:, c * T + 512 * th:c * T + 512 * th + 512],
                                start=(c == 0), stop=(c == CD - 1))
                            if c == CD - 1:
                                PE.inc(ins)
                                R(("omm", j, otl, th), PE.n)
            R("oproj_pe_end", PE.n)

        block.tensor(sec_mm)

        def sec_vec(v):
            for j in range(8):
                par = j % 2
                for otl in (0, 1):
                    c = j * 2 + otl
                    for th in (0, 1):
                        b = 4 * par + otl * 2 + th
                        W(v, PE, ("omm", j, otl, th))
                        DV.inc(v.tensor_tensor(
                            x_res[:, c * T + 512 * th:c * T + 512 * th + 512],
                            pb[b][:, :],
                            x_res[:, c * T + 512 * th:c * T + 512 * th + 512],
                            ALU.add))
                        R(("oe", j, otl, th), DV.n)

        block.vector(sec_vec)
        for par in (0, 1):
            for otl in (0, 1):
                for th in (0, 1):
                    bank_free[4 * par + otl * 2 + th] = (DV, ("oe", 6 + par, otl, th))

    def emit_store(x_ready):
        def store_c(eng, c, sem):
            sv = x_ready(c)
            if sv is not None:
                W(eng, sv[0], sv[1])
            sem.inc(eng.dma_start(
                out=out_d.ap()[c * 128:(c + 1) * 128, :],
                in_=x_res[:, c * T:(c + 1) * T]), 16)

        def sec_sy(sy):
            for c in range(0, CD, 2):
                store_c(sy, c, SX)
            R("store_sx", SX.n)
            if real:
                sy.wait_ge(SX.h, rec["store_sx"])
                sy.wait_ge(GX.h, rec["store_gx"])

        def sec_gp(g):
            for c in range(1, CD, 2):
                store_c(g, c, GX)
            R("store_gx", GX.n)
            if real:
                g.wait_ge(SX.h, rec["store_sx"])
                g.wait_ge(GX.h, rec["store_gx"])

        block.sync(sec_sy)
        block.gpsimd(sec_gp)

    gp1 = emit_ln("ln1", lambda c: (SX if c < 8 else GX, ("xg", c // 8)), None)
    emit_ffn("f1", "ln1", None, gp_pre=gp1)
    if _BUILD_UPTO >= 2:
        gp2 = emit_ln("ln2", lambda c: (DV, ("ffnx", "f1", 1, c // 8, c % 8)),
                      (PE, ("ffny", "f1")))
        emit_qkv("ln2", None, gp_pre=gp2)
    if _BUILD_UPTO >= 3:
        emit_attn()
        emit_oproj()
    if _BUILD_UPTO >= 4:
        gp3 = emit_ln("ln3", lambda c: (DV, ("oe", c // 2, c % 2, 1)),
                      (PE, "oproj_pe_end"))
        emit_ffn("f2", "ln3", None, skip_first_gates=True, gp_pre=gp3)

    if _BUILD_UPTO >= 4:
        emit_store(lambda c: (DV, ("ffnx", "f2", 1, c // 8, c % 8)))
    elif _BUILD_UPTO == 3:
        emit_store(lambda c: (DV, ("oe", c // 2, c % 2, 1)))
    else:
        emit_store(lambda c: (DV, ("ffnx", "f1", 1, c // 8, c % 8)))

    if _BUILD_UPTO == 2:
        def sec_tail(g):
            if real:
                g.wait_ge(CC.h, 5)
        block.gpsimd(sec_tail)

    return {i: s.n for i, s in enumerate(sems)}


def build_program():
    nc = bass.Bass("TRN2", target_bir_lowering=False)

    x_d = nc.dram_tensor("x", [DM, T], F32, kind="ExternalInput")
    wqkvT_d = nc.dram_tensor("wqkvT", [DM, 3 * DM], F32, kind="ExternalInput")
    woutT_d = nc.dram_tensor("woutT", [DM, DM], F32, kind="ExternalInput")
    wg_d = nc.dram_tensor("wg", [DM, IN], I8, kind="ExternalInput")
    wu_d = nc.dram_tensor("wu", [DM, IN], I8, kind="ExternalInput")
    wd_d = nc.dram_tensor("wd", [IN, DM], I8, kind="ExternalInput")
    sg_d = nc.dram_tensor("sg", [128, NI], F32, kind="ExternalInput")
    su_d = nc.dram_tensor("su", [128, NI], F32, kind="ExternalInput")
    sd_d = nc.dram_tensor("sd", [128, CD], F32, kind="ExternalInput")
    out_d = nc.dram_tensor("out", [DM, T], F32, kind="ExternalOutput")

    k_ins = [nc.dram_tensor(f"k_in{i}", [1024, T], BF16) for i in range(2)]
    k_outs = [nc.dram_tensor(f"k_out{i}", [NCORES * 1024, T], BF16,
                             addr_space="Shared") for i in range(2)]
    v_ins = [nc.dram_tensor(f"v_in{i}", [T, 512], BF16) for i in range(4)]
    v_outs = [nc.dram_tensor(f"v_out{i}", [NCORES * T, 512], BF16,
                             addr_space="Shared") for i in range(4)]

    with ExitStack() as stk:
        e = stk.enter_context
        x_res = e(nc.sbuf_tensor("x_res", [128, CD * T], F32))
        y_bf = e(nc.sbuf_tensor("y_bf", [128, CD * T], BF16))
        slabA = e(nc.sbuf_tensor("slabA", [128, 32768], BF16))
        slabW = e(nc.sbuf_tensor("slabW", [128, 16384], BF16))
        ln_scr = e(nc.sbuf_tensor("ln_scr", [128, 1024], F32))
        stats = e(nc.sbuf_tensor("stats", [1, 2048], F32))
        gtmp = e(nc.sbuf_tensor("gtmp", [128, 1024], BF16))
        sg_s = e(nc.sbuf_tensor("sg_s", [128, NI], F32))
        su_s = e(nc.sbuf_tensor("su_s", [128, NI], F32))
        sd_s = e(nc.sbuf_tensor("sd_s", [128, CD], F32))
        sdh_s = e(nc.sbuf_tensor("sdh_s", [128, CD], F32))
        ones_b = e(nc.sbuf_tensor("ones_b", [128, 1], BF16))
        ones_f = e(nc.sbuf_tensor("ones_f", [128, 1], F32))
        onesk1 = e(nc.sbuf_tensor("onesk1", [1, 128], F32))
        onesk1b = e(nc.sbuf_tensor("onesk1b", [1, 128], BF16))
        cst = e(nc.sbuf_tensor("cst", [1, 2], F32))
        pp = [e(nc.psum_tensor(f"pp{i}", [128, 1024], F32)) for i in range(4)]
        pb = [pp[i // 2][:, (i % 2) * 512:(i % 2) * 512 + 512] for i in range(8)]

        tensors = (x_res, y_bf, slabA, slabW, ln_scr, stats, gtmp, sg_s, su_s,
                   sd_s, sdh_s, ones_b, ones_f, onesk1, onesk1b, cst, pb, pp,
                   x_d, wqkvT_d, woutT_d, wg_d, wu_d, wd_d, sg_d, su_d, sd_d,
                   out_d, k_ins, k_outs, v_ins, v_outs)

        rec = {}
        SEM_NAMES = ["SX", "SC", "KB", "VA", "VB", "VC", "VD",
                     "WA", "WB", "QA", "QB", "DA", "DB", "DC", "DD",
                     "KR0", "KR1", "KR2", "VR0", "VR1", "VR2",
                     "CC", "PE", "AC", "DV", "GP", "KN", "GX", "QT0", "QT1",
                     "UA", "UB", "KM", "VG"]
        plan_sems = tuple(PlanSem() for _ in SEM_NAMES)
        plan_counts = emit_all(_FakeBlock(), nc, tensors, plan_sems, rec,
                               real=False)

        real_sems = tuple(Sem(stk, nc, n) for n in SEM_NAMES)
        block = e(nc.Block())
        real_counts = emit_all(block, nc, tensors, real_sems, rec, real=True)
        assert plan_counts == real_counts, (plan_counts, real_counts)

    return nc


_NC_CACHE = None


def _enable_ldw_opt():
    try:
        import libneuronxla.libncc as ncc
        for i, f in enumerate(ncc.NEURON_CC_FLAGS):
            if f.startswith('--internal-backend-options='):
                ncc.NEURON_CC_FLAGS[i] = f.replace(
                    '--enable-ldw-opt=false', '--enable-ldw-opt=true')
    except Exception:
        pass


def _get_program():
    global _NC_CACHE
    if _NC_CACHE is None:
        _enable_ldw_opt()
        _NC_CACHE = build_program()
    return _NC_CACHE


def kernel(L0_attn_x, L0_attn_w_qkv, L0_attn_w_out, L0_ffn_w_gate,
           L0_ffn_w_up, L0_ffn_w_down, L0_ffn_s_gate, L0_ffn_s_up,
           L0_ffn_s_down):
    x = np.asarray(L0_attn_x)
    wqkvT = np.ascontiguousarray(np.asarray(L0_attn_w_qkv).T)
    woutT = np.ascontiguousarray(np.asarray(L0_attn_w_out).T)
    wg = np.ascontiguousarray(np.asarray(L0_ffn_w_gate))
    wu = np.ascontiguousarray(np.asarray(L0_ffn_w_up))
    wd = np.ascontiguousarray(np.asarray(L0_ffn_w_down))
    sg = np.ascontiguousarray(np.asarray(L0_ffn_s_gate).reshape(NI, 128).T)
    su = np.ascontiguousarray(np.asarray(L0_ffn_s_up).reshape(NI, 128).T)
    sd = np.ascontiguousarray(np.asarray(L0_ffn_s_down).reshape(CD, 128).T)

    nc = _get_program()
    in_maps = []
    for c in range(NCORES):
        xc = np.ascontiguousarray(x[0, c * T:(c + 1) * T, :].T)
        in_maps.append({"x": xc, "wqkvT": wqkvT, "woutT": woutT,
                        "wg": wg, "wu": wu, "wd": wd,
                        "sg": sg, "su": su, "sd": sd})
    res = run_bass_kernel_spmd(nc, in_maps, core_ids=list(range(NCORES)))
    out = np.concatenate([np.asarray(r["out"]).T for r in res.results], axis=0)
    return np.ascontiguousarray(out).reshape(1, S, DM)
